# version 07
# baseline (speedup 1.0000x reference)
import numpy as np
import ml_dtypes
import concourse.bass as bass
import concourse.mybir as mybir
from concourse.bass_utils import run_bass_kernel_spmd

F32 = mybir.dt.float32
I32 = mybir.dt.int32
BF16 = mybir.dt.bfloat16
ALU = mybir.AluOpType
AF = mybir.ActivationFunctionType
AX = mybir.AxisListType.X
BF = ml_dtypes.bfloat16

NCORES = 8
NB = 2
T = 2048
NTOK = NB * T
D = 1024
DC = D // 128
HID = 3072
HC = HID // 128
NT = NTOK // 512
QMAX = 32767.0
EPS = 1e-5
NHEAD = 16
DH = 64
CHK = 128
NCH = T // CHK

CHAINED = ("vector", "scalar", "gpsimd")
DEBUG_L5 = False


class Prog:

    ENGS = ("sync", "tensor", "vector", "scalar", "gpsimd")

    def __init__(self, name):
        self.name = name
        self.nc = bass.Bass("TRN2", target_bir_lowering=False, debug=False)
        self.thunks = {e: [] for e in self.ENGS}
        self.tick = {e: 0 for e in self.ENGS}
        self.NLANES = 2
        self.dval = {(e, l): 0 for e in self.ENGS for l in range(self.NLANES)}
        self.dcnt = {e: 0 for e in self.ENGS}
        self.cache = {}
        self.psum_pool = [self.nc.alloc_psum_tensor(f"psb{j}", [128, 512], F32).ap()
                          for j in range(8)]
        self._arena = None
        self._arena_off = 0

    def arena_reset(self, ncols=14336):
        if self._arena is None:
            self._arena = self.sbuf("arena", [128, ncols], F32)
            self._arena_cols = ncols
        self._arena_off = 0

    def arena(self, ncols, dtype=F32, rows=128):
        fcols = (ncols * mybir.dt.size(dtype) + 3) // 4
        off = self._arena_off
        assert off + fcols <= self._arena_cols, (off, fcols, self._arena_cols)
        self._arena_off += fcols
        view = self._arena[0:rows, off:off + fcols]
        if dtype != F32:
            view = view.bitcast(dtype)[0:rows, 0:ncols]
        return view

    def last(self, eng):
        return ("e", eng, self.tick[eng])

    def dlast(self, queue):
        lane = (self.dcnt[queue] - 1) % self.NLANES
        return ("d", (queue, lane), self.dval[(queue, lane)])

    def phase_barrier(self):
        tags = [("e", e, self.tick[e]) for e in self.ENGS if self.tick[e] > 0]
        tags += [("d", k, v) for k, v in self.dval.items() if v > 0]
        return tuple(tags)

    def dram_in(self, name, shape, dtype=F32):
        return self.nc.dram_tensor(name, list(shape), dtype, kind="ExternalInput").ap()

    def dram_out(self, name, shape, dtype=F32):
        return self.nc.dram_tensor(name, list(shape), dtype, kind="ExternalOutput").ap()

    def dram_tmp(self, name, shape, dtype=F32):
        return self.nc.dram_tensor(name, list(shape), dtype, kind="Internal").ap()

    def sbuf(self, name, shape, dtype=F32):
        return self.nc.alloc_sbuf_tensor(name, list(shape), dtype).ap()

    def const_sb(self, name, arr, queue="gpsimd"):
        if name not in self.cache:
            arr = np.ascontiguousarray(arr)
            dt = {np.dtype(np.float32): F32, np.dtype(BF): BF16,
                  np.dtype(np.int32): I32}[arr.dtype]
            dr = self.nc.inline_tensor(arr, name).ap()
            sb = self.sbuf(name + "_sb", list(arr.shape), dt)
            tag = self.dma(queue, sb[:, :], dr[:, :])
            self.cache[name] = (sb[:, :], tag)
        return self.cache[name]

    def op(self, eng, fn, waits=(), tag=True):
        assert tag or eng == "tensor"
        if tag:
            self.tick[eng] += 1
        tick = self.tick[eng]
        chain = eng in CHAINED and tick > 1
        waits = tuple(waits)

        def thunk(engine, waits=waits, tick=tick, fn=fn, chain=chain, eng=eng,
                  tag=tag):
            if chain and tag:
                engine.wait_ge(self.sems[eng], tick - 1)
            for kind, src, val in waits:
                if kind == "e" and src == eng and val >= tick:
                    raise RuntimeError(f"self-deadlock {eng} {val}>={tick}")
                sem = self.sems[src] if kind == "e" else self.dsems[src]
                engine.wait_ge(sem, val)
            inst = fn(engine)
            if tag:
                inst.then_inc(self.sems[eng], 1)

        self.thunks[eng].append(thunk)
        return ("e", eng, tick)

    def dma(self, queue, out, in_, waits=()):
        lane = self.dcnt[queue] % self.NLANES
        self.dcnt[queue] += 1
        self.dval[(queue, lane)] += 16
        val = self.dval[(queue, lane)]
        waits = tuple(waits)

        def thunk(engine, waits=waits, out=out, in_=in_, queue=queue,
                  lane=lane, val=val):
            if val > 16:
                engine.wait_ge(self.dsems[(queue, lane)], val - 16)
            for kind, src, v in waits:
                sem = self.sems[src] if kind == "e" else self.dsems[src]
                engine.wait_ge(sem, v)
            engine.dma_start(out=out, in_=in_).then_inc(
                self.dsems[(queue, lane)], 16)

        self.thunks[queue].append(thunk)
        return ("d", (queue, lane), val)

    def finalize(self, final_waits):
        nc = self.nc
        import contextlib
        with contextlib.ExitStack() as stack:
            self.sems = {e: stack.enter_context(nc.semaphore(f"tk_{e}"))
                         for e in self.ENGS}
            self.dsems = {(e, l): stack.enter_context(nc.semaphore(f"dm_{e}{l}"))
                          for e in self.ENGS for l in range(self.NLANES)}
            block = stack.enter_context(nc.Block())

            def final_thunk(engine):
                for kind, src, val in final_waits:
                    sem = self.sems[src] if kind == "e" else self.dsems[src]
                    engine.wait_ge(sem, val)

            self.thunks["sync"].append(final_thunk)
            for eng_name, runner in (("sync", block.sync), ("tensor", block.tensor),
                                     ("vector", block.vector), ("scalar", block.scalar),
                                     ("gpsimd", block.gpsimd)):
                thunks = self.thunks[eng_name]
                if not thunks:
                    continue

                def body(engine, thunks=thunks):
                    for t in thunks:
                        t(engine)

                runner(body)
        return nc



def emit_scale(p, mall_dr, ncols, scol, rcol, ps, qmax=QMAX, floor=1e-12, pfx="sc"):
    mrow = p.sbuf(pfx + "_mrow", [1, ncols], F32)
    t_in = p.dma("sync", mrow[:, :], mall_dr[:, :])
    cell = p.sbuf(pfx + "_cell", [1, 1], F32)
    rcell = p.sbuf(pfx + "_rcell", [1, 1], F32)
    p.op("vector", lambda v: v.tensor_reduce(cell[:, :], mrow[:, :], AX, ALU.max),
         waits=(t_in,))
    t2 = p.op("vector", lambda v: v.tensor_scalar(
        cell[:, :], cell[:, :], 1.0 / qmax, floor, ALU.mult, ALU.max))
    t3 = p.op("vector", lambda v: v.reciprocal(rcell[:, :], cell[:, :]))
    onesr, t_o = p.const_sb("ones_1x128_f32", np.ones((1, 128), np.float32))
    tm1 = p.op("tensor", lambda t: t.matmul(ps[:, 0:1], onesr, cell[:, :],
                                            start=True, stop=True),
               waits=(t2, t3, t_o))
    tm2 = p.op("tensor", lambda t: t.matmul(ps[:, 1:2], onesr, rcell[:, :],
                                            start=True, stop=True))
    p.op("vector", lambda v: v.tensor_copy(scol[:, :], ps[:, 0:1]), waits=(tm1,))
    t5 = p.op("vector", lambda v: v.tensor_copy(rcol[:, :], ps[:, 1:2]), waits=(tm2,))
    return t5


def emit_absmax_cols_out(p, cols_sb, ntens, ps, out_dr, waits=(), pfx="amx"):
    cmb = p.sbuf(pfx + "_cmb", [128, ntens], F32)
    tprev = None
    for k in range(ntens):
        tprev = p.op("vector", lambda v, k=k: v.scalar_tensor_tensor(
            cmb[:, k:k + 1], cols_sb[:, 2 * k + 1:2 * k + 2], -1.0,
            cols_sb[:, 2 * k:2 * k + 1], ALU.mult, ALU.max),
            waits=waits if k == 0 else ())
    identf, tid = p.const_sb("ident128_f32", np.eye(128, dtype=np.float32))
    tt = p.op("tensor", lambda t: t.transpose(ps[0:ntens, 0:128], cmb[:, :],
                                              identf), waits=(tprev, tid))
    mcol = p.sbuf(pfx + "_mcol", [ntens, 1], F32)
    tr = p.op("vector", lambda v: v.tensor_reduce(
        mcol[:, :], ps[0:ntens, 0:128], AX, ALU.max), waits=(tt,))
    return p.dma("sync", out_dr[:, :], mcol[:, :], waits=(tr,))


def emit_ln_rows(p, xtile_dma, arow, brow, ps_mu, ps_m2, waits=(), pfx="ln"):
    onescol, t_oc = p.const_sb("onescol_bf", np.ones((128, 1), BF))
    epsc, t_eps = p.const_sb("eps_col", np.full((128, 1), EPS, np.float32))
    xhat = p.arena(512, BF16)
    sq = p.arena(512, BF16)
    mrow = p.arena(512, F32, rows=1)
    vrow = p.arena(512, F32, rows=1)
    prev_mm = ()
    t6 = None
    for nt in range(NT):
        for c in range(DC):
            xt, t_x = xtile_dma(nt, c, waits if (nt == 0 and c == 0) else ())
            tv = p.op("vector", lambda v, xt=xt: v.tensor_copy(xhat, xt),
                      waits=(t_x,) + prev_mm)
            ts_ = p.op("scalar", lambda s: s.activation(
                out=sq, in_=xhat, func=AF.Square),
                waits=(tv,) + prev_mm)
            wv1 = (tv, t_oc) if t6 is None else (tv, t_oc, t6)
            tm1 = p.op("tensor", lambda t, c=c: t.matmul(
                ps_mu[0:1, :], onescol, xhat,
                start=(c == 0), stop=(c == DC - 1)), waits=wv1)
            tm2 = p.op("tensor", lambda t, c=c: t.matmul(
                ps_m2[0:1, :], onescol, sq,
                start=(c == 0), stop=(c == DC - 1)), waits=(ts_,))
            prev_mm = (tm1, tm2)
        sl = slice(nt * 512, (nt + 1) * 512)
        p.op("vector", lambda v: v.tensor_scalar(
            mrow, ps_mu[0:1, :], 1.0 / D, None, ALU.mult), waits=prev_mm)
        p.op("vector", lambda v: v.tensor_tensor(
            vrow, mrow, mrow, ALU.mult))
        tv3 = p.op("vector", lambda v: v.scalar_tensor_tensor(
            vrow, ps_m2[0:1, :], 1.0 / D, vrow, ALU.mult, ALU.subtract))
        t4 = p.op("scalar", lambda s: s.activation(
            out=vrow, in_=vrow, func=AF.Ln, bias=epsc[0:1, :]),
            waits=(tv3, t_eps))
        t5 = p.op("scalar", lambda s, sl=sl: s.activation(
            out=arow[:, sl], in_=vrow, func=AF.Exp, scale=-0.5))
        t6 = p.op("vector", lambda v, sl=sl: v.scalar_tensor_tensor(
            brow[:, sl], mrow, -1.0, arow[:, sl], ALU.mult, ALU.mult),
            waits=(t5,))
    return t6


def emit_ab_bcast(p, arow, brow, nt, ps_a, ps_b, waits=()):
    onesr, t_o = p.const_sb("ones_1x128_f32", np.ones((1, 128), np.float32))
    sl = slice(nt * 512, (nt + 1) * 512)
    t1 = p.op("tensor", lambda t: t.matmul(ps_a[:, :], onesr, arow[:, sl],
                                           start=True, stop=True),
              waits=tuple(waits) + (t_o,))
    t2 = p.op("tensor", lambda t: t.matmul(ps_b[:, :], onesr, brow[:, sl],
                                           start=True, stop=True), waits=tuple(waits))
    return t1, t2



def emit_lnmax(p, mk_xtile, arow, brow, m_out, ps, t_rows, pfx="lm"):
    t1s = p.arena(512, F32)
    xns = p.arena(512, F32)
    junk = p.arena(512, BF16)
    NA = NT * DC
    amax = p.sbuf(pfx + "_amax", [128, NA], F32)
    amin = p.sbuf(pfx + "_amin", [128, NA], F32)
    prev = None
    for nt in range(NT):
        wv = (t_rows,) if prev is None else (prev,)
        ta, tb = emit_ab_bcast(p, arow, brow, nt, ps[2], ps[3], waits=wv)
        for c in range(DC):
            k = nt * DC + c
            xap, t_x = mk_xtile(nt, c)
            p.op("vector", lambda v, xap=xap: v.tensor_tensor(
                t1s, xap, ps[2][:, :], ALU.mult), waits=(ta, t_x))
            p.op("vector", lambda v: v.tensor_tensor(
                xns, t1s, ps[3][:, :], ALU.add), waits=(tb,))
            p.op("vector", lambda v, k=k: v.tensor_scalar(
                junk, xns, 1.0, None, ALU.mult, ALU.max,
                accum_out=amax[:, k:k + 1]))
            prev = p.op("vector", lambda v, k=k: v.tensor_scalar(
                junk, xns, 1.0, None, ALU.mult, ALU.min,
                accum_out=amin[:, k:k + 1]))
    acc2 = p.sbuf(pfx + "_acc2", [128, 2], F32)
    p.op("vector", lambda v: v.tensor_reduce(
        acc2[:, 0:1], amax[:, :], AX, ALU.max), waits=(prev,))
    tcp = p.op("vector", lambda v: v.tensor_reduce(
        acc2[:, 1:2], amin[:, :], AX, ALU.min))
    return emit_absmax_cols_out(p, acc2, 1, ps[4][:, 0:128], m_out, waits=(tcp,),
                                pfx=pfx + "_amx")


def mk_dram_streamer(p, src_dr, queue="sync", first_waits=()):
    bufs = [p.arena(512, F32) for j in range(2)]
    state = {"j": 0, "rd": [None, None]}

    def mk_xtile(nt, c, wv=()):
        j = state["j"]
        state["j"] = 1 - j
        wl = list(wv)
        if state["rd"][j] is None:
            wl += list(first_waits)
        else:
            wl.append(state["rd"][j])
        tag = p.dma(queue, bufs[j][:, :],
                    src_dr[c * 128:(c + 1) * 128, nt * 512:(nt + 1) * 512],
                    waits=tuple(wl))
        state["rd"][j] = ("e", "vector", p.tick["vector"] + 1)
        return bufs[j][:, :], tag

    return mk_xtile


def build_L1():
    p = Prog("L1")
    xT = p.dram_in("xT", [D, NTOK])
    m1 = p.dram_out("m1", [1, 1])
    ab = p.dram_out("ab", [2, NTOK])
    p.arena_reset()
    arow = p.sbuf("arow", [1, NTOK], F32)
    brow = p.sbuf("brow", [1, NTOK], F32)
    ps = p.psum_pool
    mk1 = mk_dram_streamer(p, xT)
    t_rows = emit_ln_rows(p, mk1, arow, brow, ps[0], ps[1])
    t_ab1 = p.dma("sync", ab[0:1, :], arow[:, :], waits=(t_rows,))
    t_ab2 = p.dma("sync", ab[1:2, :], brow[:, :], waits=(t_rows,))
    mk2 = mk_dram_streamer(p, xT)
    t_m1 = emit_lnmax(p, mk2, arow, brow, m1, ps, t_rows)
    return p.finalize([t_ab1, t_ab2, t_m1])



def emit_dequant(p, wnat, wT, wq_sb, nrows, nin, ps_t, pfx, waits=()):
    NBLK = nin // 32
    wnat_sb, bm, sc, wqn, qq, identbf, tid = p.dq_scr
    ntile = nrows // 128
    prev_tq = None
    prev_tr = {}
    for ot in range(ntile):
        wv = list(waits)
        if prev_tq is not None:
            wv.append(prev_tq)
        tl = p.dma("sync", wnat_sb, wnat[ot * 128:(ot + 1) * 128, :],
                   waits=tuple(wv))
        p.op("vector", lambda v: v.tensor_reduce(
            bm.rearrange("p (g b) -> p g b", b=1),
            wnat_sb.rearrange("p (g b) -> p g b", b=32),
            AX, ALU.max, apply_absolute_value=True), waits=(tl,))
        p.op("vector", lambda v: v.tensor_scalar(
            sc[:, 0:NBLK], bm, 1.0 / 7.0, 1e-12, ALU.mult, ALU.max))
        p.op("vector", lambda v: v.reciprocal(sc[:, NBLK:2 * NBLK], sc[:, 0:NBLK]))
        rb = sc[:, NBLK:2 * NBLK].rearrange("p (g b) -> p g b", b=1) \
            .broadcast_to((128, NBLK, 32))
        sb_ = sc[:, 0:NBLK].rearrange("p (g b) -> p g b", b=1) \
            .broadcast_to((128, NBLK, 32))
        w3 = wnat_sb.rearrange("p (g b) -> p g b", b=32)
        p.op("vector", lambda v: v.tensor_tensor(
            wnat_sb.rearrange("p (g b) -> p g b", b=32), w3, rb, ALU.mult))
        prev_tq = p.op("vector", lambda v: v.tensor_scalar(
            qq, wnat_sb, 1.0, None, ALU.mult))
        tw = p.op("vector", lambda v: v.tensor_tensor(
            wqn.rearrange("p (g b) -> p g b", b=32),
            qq.rearrange("p (g b) -> p g b", b=32), sb_, ALU.mult))
        for c in range(nin // 128):
            bank = ps_t[c % 2][:, 0:64].bitcast(BF16)[:, 0:128]
            wvt = (tw, tid) + (prev_tr.get(c % 2, ()) or ())
            tt = p.op("tensor", lambda t, bank=bank, c=c: t.transpose(
                bank, wqn[:, c * 128:(c + 1) * 128], identbf), waits=wvt)
            te = p.op("vector", lambda v, bank=bank, c=c, ot=ot: v.tensor_copy(
                wq_sb[:, c * nrows + ot * 128: c * nrows + (ot + 1) * 128],
                bank), waits=(tt,))
            prev_tr[c % 2] = (te,)
    return te


def prep_dq_scr(p, nrows, nin):
    NBLK = nin // 32
    identbf, tid = p.const_sb("identbf", np.eye(128, dtype=BF))
    p.dq_scr = (p.arena(nin, F32), p.arena(NBLK, F32), p.arena(2 * NBLK, F32),
                p.arena(nin, BF16), p.arena(nin, I32), identbf, tid)


def emit_xq(p, xT, ab, m1all, xq, ps, pfx=""):
    s1col = p.sbuf(pfx + "s1col", [128, 1], F32)
    a1col = p.sbuf(pfx + "a1col", [128, 1], F32)
    t_s1 = emit_scale(p, m1all, NCORES, s1col, a1col, ps[4][:, 0:2], pfx=pfx + "s1")
    arow = p.sbuf(pfx + "arow", [1, NTOK], F32)
    brow = p.sbuf(pfx + "brow", [1, NTOK], F32)
    tab1 = p.dma("sync", arow[:, :], ab[0:1, :])
    tab2 = p.dma("sync", brow[:, :], ab[1:2, :])
    xtile = [p.arena(512, F32) for _ in range(2)]
    t1s = p.arena(512, F32)
    xns = p.arena(512, F32)
    qi = p.arena(512, I32)
    prev = None
    prev_rd = [None, None]
    for nt in range(NT):
        wv = [tab1, tab2] if prev is None else [prev]
        ta, tb = emit_ab_bcast(p, arow, brow, nt, ps[2], ps[3], waits=tuple(wv))
        for c in range(DC):
            j = c % 2
            wvx = [] if prev_rd[j] is None else [prev_rd[j]]
            t_x = p.dma("sync", xtile[j],
                        xT[c * 128:(c + 1) * 128, nt * 512:(nt + 1) * 512],
                        waits=tuple(wvx))
            prev_rd[j] = p.op("vector", lambda v, j=j: v.tensor_tensor(
                t1s, xtile[j], ps[2][:, :], ALU.mult), waits=(ta, t_x))
            p.op("vector", lambda v: v.tensor_tensor(
                xns, t1s, ps[3][:, :], ALU.add), waits=(tb,))
            p.op("vector", lambda v: v.tensor_scalar(
                qi, xns, a1col[:, :], None, ALU.mult), waits=(t_s1,))
            sl = slice(c * NTOK + nt * 512, c * NTOK + (nt + 1) * 512)
            prev = p.op("vector", lambda v, sl=sl: v.tensor_scalar(
                xq[:, sl], qi, s1col[:, :], None, ALU.mult))
    return prev, s1col, a1col


def build_L2(name="L2"):
    p = Prog(name)
    xT = p.dram_in("xT", [D, NTOK])
    ab = p.dram_in("ab", [2, NTOK])
    m1all = p.dram_in("m1all", [1, NCORES])
    wupN = p.dram_in("wupN", [HID, D])
    wupT = p.dram_in("wupT", [D, HID])
    wgateN = p.dram_in("wgateN", [HID, D])
    wgateT = p.dram_in("wgateT", [D, HID])
    up_raw = p.dram_out("up_raw", [HID, NTOK])
    gate_raw = p.dram_out("gate_raw", [HID, NTOK])
    m23 = p.dram_out("m23", [1, 2])

    ps = p.psum_pool
    p.arena_reset(12800)
    xq = p.sbuf("xq", [128, DC * NTOK], BF16)
    t_xq, _, _ = emit_xq(p, xT, ab, m1all, xq, ps)

    prep_dq_scr(p, HID, D)

    wq = p.sbuf("wq", [128, DC * HID], BF16)
    out_sb = p.sbuf("out_sb", [128, 2 * 512], F32)
    junk = p.sbuf("junkbf", [128, 512], BF16)
    NACC = HC * NT
    amax = [p.sbuf(f"amax{w}", [128, NACC], F32) for w in range(2)]
    amin = [p.sbuf(f"amin{w}", [128, NACC], F32) for w in range(2)]

    store_last = None
    mm_waits_extra = []
    slot_store = [None, None]
    evac_last = None
    for wi, (wN, wTd, raw_dr) in enumerate(((wupN, wupT, up_raw),
                                            (wgateN, wgateT, gate_raw))):
        t_w = emit_dequant(p, wN, wTd, wq, HID, D, (ps[4], ps[5]), f"dq{wi}",
                           waits=tuple(mm_waits_extra))
        evac_prev = [None] * NT
        mm_all_last = None
        for ot in range(HC):
            mm_last = [None] * NT
            for c in range(DC):
                wsl = wq[:, c * HID + ot * 128: c * HID + (ot + 1) * 128]
                for nt in range(NT):
                    wv = []
                    if c == 0:
                        wv = [t_w, t_xq]
                        if evac_prev[nt] is not None:
                            wv.append(evac_prev[nt])
                    xsl = xq[:, c * NTOK + nt * 512: c * NTOK + (nt + 1) * 512]
                    mm_last[nt] = p.op(
                        "tensor", lambda t, wsl=wsl, xsl=xsl, c=c, nt=nt:
                        t.matmul(ps[nt][:, :], wsl, xsl,
                                 start=(c == 0), stop=(c == DC - 1)),
                        waits=tuple(wv), tag=(c == DC - 1))
            mm_all_last = mm_last[-1]
            for nt in range(NT):
                k = ot * NT + nt
                slot = nt % 2
                osl = slice(slot * 512, (slot + 1) * 512)
                wv = [mm_last[nt]]
                if slot_store[slot] is not None:
                    wv.append(slot_store[slot])
                p.op("vector", lambda v, osl=osl, nt=nt, k=k, wi=wi: v.tensor_scalar(
                    out_sb[:, osl], ps[nt][:, :], 1.0, None,
                    ALU.mult, ALU.max, accum_out=amax[wi][:, k:k + 1]),
                    waits=tuple(wv))
                evac_prev[nt] = p.op(
                    "vector", lambda v, osl=osl, k=k, wi=wi: v.tensor_scalar(
                        junk[:, :], out_sb[:, osl], 1.0, None,
                        ALU.mult, ALU.min, accum_out=amin[wi][:, k:k + 1]))
                evac_last = evac_prev[nt]
                slot_store[slot] = p.dma(
                    "scalar", raw_dr[ot * 128:(ot + 1) * 128, nt * 512:(nt + 1) * 512],
                    out_sb[:, osl], waits=(evac_prev[nt],))
        store_last = slot_store[(NT - 1) % 2]
        mm_waits_extra = [mm_all_last]
    acc = p.sbuf("acc", [128, 4], F32)
    tred = None
    for wi in range(2):
        p.op("vector", lambda v, wi=wi: v.tensor_reduce(
            acc[:, 2 * wi:2 * wi + 1], amax[wi][:, :], AX, ALU.max),
            waits=(evac_last,) if wi == 0 else ())
        tred = p.op("vector", lambda v, wi=wi: v.tensor_reduce(
            acc[:, 2 * wi + 1:2 * wi + 2], amin[wi][:, :], AX, ALU.min))
    t_m = emit_absmax_cols_out(p, acc[:, :], 2, ps[5][:, 0:128], m23,
                               waits=(tred,))
    return p.finalize([store_last, t_m])




def emit_cell_max(p, mall_dr, ncols, pfx, waits=()):
    mrow = p.sbuf(pfx + "_mrow", [1, ncols], F32)
    t_in = p.dma("sync", mrow[:, :], mall_dr[:, :], waits=waits)
    cell = p.sbuf(pfx + "_cell", [1, 1], F32)
    t = p.op("vector", lambda v: v.tensor_reduce(cell[:, :], mrow[:, :], AX, ALU.max),
             waits=(t_in,))
    return cell, t


def emit_cell_scale(p, cell_in, pfx, qmax=QMAX, floor=1e-12):
    sc = p.sbuf(pfx + "_s", [1, 1], F32)
    rc = p.sbuf(pfx + "_r", [1, 1], F32)
    p.op("vector", lambda v: v.tensor_scalar(
        sc[:, :], cell_in[:, :], 1.0 / qmax, floor, ALU.mult, ALU.max))
    t = p.op("vector", lambda v: v.reciprocal(rc[:, :], sc[:, :]))
    return sc, rc, t


def emit_bcast_cols(p, cells, cols, ps, waits=()):
    onesr, t_o = p.const_sb("ones_1x128_f32", np.ones((1, 128), np.float32))
    last = None
    for j, (cell, col) in enumerate(zip(cells, cols)):
        tm = p.op("tensor", lambda t, j=j, cell=cell: t.matmul(
            ps[:, j:j + 1], onesr, cell[:, :], start=True, stop=True),
            waits=tuple(waits) + (t_o,))
        last = p.op("vector", lambda v, j=j, col=col: v.tensor_copy(
            col[:, :], ps[:, j:j + 1]), waits=(tm,))
    return last



def emit_stream_maxmin_out(p, acc, out_dr, waits, pfx="sm"):
    return emit_absmax_cols_out(p, acc, 1, p.psum_pool[5][:, 0:128], out_dr,
                                waits=waits, pfx=pfx)


def build_L3(name="L3"):
    p = Prog(name)
    gate_raw = p.dram_in("gate_raw", [HID, NTOK])
    m3all = p.dram_in("m3all", [1, NCORES])
    GR_raw = p.dram_out("GR_raw", [HID, NTOK])
    m4 = p.dram_out("m4", [1, 1])
    ps = p.psum_pool
    p.arena_reset()

    c3, t0 = emit_cell_max(p, m3all, NCORES, "m3")
    s3c, a3c, t1 = emit_cell_scale(p, c3, "s3")
    s3col = p.sbuf("s3col", [128, 1], F32)
    a3col = p.sbuf("a3col", [128, 1], F32)
    t_sc = emit_bcast_cols(p, [s3c, a3c], [s3col, a3col], ps[6][:, 0:2],
                           waits=(t1,))

    gin = [p.sbuf(f"gin{j}", [128, NTOK], F32) for j in range(2)]
    gout = [p.sbuf(f"gout{j}", [128, NTOK], F32) for j in range(2)]
    G = p.sbuf("G", [128, NTOK], I32)
    sg = p.sbuf("sg", [128, NTOK], F32)
    R = p.sbuf("R", [128, NTOK], I32)
    junk = p.sbuf("junk", [128, NTOK], BF16)
    amax = p.sbuf("amax", [128, HC], F32)
    amin = p.sbuf("amin", [128, HC], F32)

    rd_prev = [None, None]
    st_prev = [None, None]
    last_v = None
    for ht in range(HC):
        j = ht % 2
        wv = [t_sc] if rd_prev[j] is None else [rd_prev[j]]
        t_in = p.dma("sync", gin[j][:, :], gate_raw[ht * 128:(ht + 1) * 128, :],
                     waits=tuple(wv))
        tG = p.op("vector", lambda v, j=j: v.tensor_scalar(
            G[:, :], gin[j][:, :], a3col[:, :], None, ALU.mult), waits=(t_in,))
        rd_prev[j] = tG
        tS = p.op("scalar", lambda s: s.activation(
            out=sg[:, :], in_=G[:, :], func=AF.Sigmoid, scale=s3col[:, :]),
            waits=(tG,))
        tR = p.op("vector", lambda v: v.tensor_scalar(
            R[:, :], sg[:, :], 65536.0, 65535.0, ALU.mult, ALU.min), waits=(tS,))
        wv2 = [tR]
        if st_prev[j] is not None:
            wv2.append(st_prev[j])
        tGR = p.op("vector", lambda v, j=j: v.tensor_tensor(
            gout[j][:, :], G[:, :], R[:, :], ALU.mult), waits=tuple(wv2))
        p.op("vector", lambda v, j=j, ht=ht: v.tensor_scalar(
            junk[:, :], gout[j][:, :], 1.0, None, ALU.mult, ALU.max,
            accum_out=amax[:, ht:ht + 1]))
        last_v = p.op("vector", lambda v, j=j, ht=ht: v.tensor_scalar(
            junk[:, :], gout[j][:, :], 1.0, None, ALU.mult, ALU.min,
            accum_out=amin[:, ht:ht + 1]))
        st_prev[j] = p.dma("scalar", GR_raw[ht * 128:(ht + 1) * 128, :],
                           gout[j][:, :], waits=(tGR,))
    acc = p.sbuf("acc", [128, 2], F32)
    p.op("vector", lambda v: v.tensor_reduce(
        acc[:, 0:1], amax[:, :], AX, ALU.max), waits=(last_v,))
    tred = p.op("vector", lambda v: v.tensor_reduce(
        acc[:, 1:2], amin[:, :], AX, ALU.min))
    t_m = emit_stream_maxmin_out(p, acc, m4, (tred,), pfx="m4o")
    return p.finalize([st_prev[0], st_prev[1], t_m])


def build_L4(name="L4"):
    p = Prog(name)
    GR_raw = p.dram_in("GR_raw", [HID, NTOK])
    up_raw = p.dram_in("up_raw", [HID, NTOK])
    m2all = p.dram_in("m2all", [1, NCORES])
    m3all = p.dram_in("m3all", [1, NCORES])
    m4all = p.dram_in("m4all", [1, NCORES])
    OP_raw = p.dram_out("OP_raw", [HID, NTOK])
    m5 = p.dram_out("m5", [1, 1])
    ps = p.psum_pool
    p.arena_reset()

    c2, _ = emit_cell_max(p, m2all, NCORES, "m2")
    s2c, a2c, _ = emit_cell_scale(p, c2, "s2")
    c3, _ = emit_cell_max(p, m3all, NCORES, "m3")
    s3c, _, _ = emit_cell_scale(p, c3, "s3")
    c4raw, _ = emit_cell_max(p, m4all, NCORES, "m4")
    c34 = p.sbuf("c34", [1, 1], F32)
    p.op("vector", lambda v: v.tensor_scalar(
        c34[:, :], s3c[:, :], 2.0 ** -16, None, ALU.mult))
    m4v = p.sbuf("m4v", [1, 1], F32)
    p.op("vector", lambda v: v.tensor_tensor(m4v[:, :], c4raw[:, :], c34[:, :], ALU.mult))
    s4c, s4r, _ = emit_cell_scale(p, m4v, "s4")
    a4p = p.sbuf("a4p", [1, 1], F32)
    t_ap = p.op("vector", lambda v: v.tensor_tensor(
        a4p[:, :], c34[:, :], s4r[:, :], ALU.mult))
    a4col = p.sbuf("a4col", [128, 1], F32)
    a2col = p.sbuf("a2col", [128, 1], F32)
    t_sc = emit_bcast_cols(p, [a4p, a2c], [a4col, a2col], ps[6][:, 0:2],
                           waits=(t_ap,))

    gin = [p.sbuf(f"gin{j}", [128, NTOK], F32) for j in range(2)]
    uin = [p.sbuf(f"uin{j}", [128, NTOK], F32) for j in range(2)]
    oout = [p.sbuf(f"oout{j}", [128, NTOK], F32) for j in range(2)]
    G2 = p.sbuf("G2", [128, NTOK], I32)
    U = p.sbuf("U", [128, NTOK], I32)
    junk = p.sbuf("junk", [128, NTOK], BF16)
    amax = p.sbuf("amax", [128, HC], F32)
    amin = p.sbuf("amin", [128, HC], F32)

    rd_prev = [None, None]
    st_prev = [None, None]
    last_v = None
    for ht in range(HC):
        j = ht % 2
        wv = [t_sc] if rd_prev[j] is None else [rd_prev[j]]
        t_g = p.dma("sync", gin[j][:, :], GR_raw[ht * 128:(ht + 1) * 128, :],
                    waits=tuple(wv))
        t_u = p.dma("gpsimd", uin[j][:, :], up_raw[ht * 128:(ht + 1) * 128, :],
                    waits=tuple(wv))
        tG = p.op("vector", lambda v, j=j: v.tensor_scalar(
            G2[:, :], gin[j][:, :], a4col[:, :], None, ALU.mult), waits=(t_g,))
        tU = p.op("vector", lambda v, j=j: v.tensor_scalar(
            U[:, :], uin[j][:, :], a2col[:, :], None, ALU.mult), waits=(t_u,))
        rd_prev[j] = tU
        wv2 = [tU]
        if st_prev[j] is not None:
            wv2.append(st_prev[j])
        tOP = p.op("vector", lambda v, j=j: v.tensor_tensor(
            oout[j][:, :], G2[:, :], U[:, :], ALU.mult), waits=tuple(wv2))
        p.op("vector", lambda v, j=j, ht=ht: v.tensor_scalar(
            junk[:, :], oout[j][:, :], 1.0, None, ALU.mult, ALU.max,
            accum_out=amax[:, ht:ht + 1]))
        last_v = p.op("vector", lambda v, j=j, ht=ht: v.tensor_scalar(
            junk[:, :], oout[j][:, :], 1.0, None, ALU.mult, ALU.min,
            accum_out=amin[:, ht:ht + 1]))
        st_prev[j] = p.dma("scalar", OP_raw[ht * 128:(ht + 1) * 128, :],
                           oout[j][:, :], waits=(tOP,))
    acc = p.sbuf("acc", [128, 2], F32)
    p.op("vector", lambda v: v.tensor_reduce(
        acc[:, 0:1], amax[:, :], AX, ALU.max), waits=(last_v,))
    tred = p.op("vector", lambda v: v.tensor_reduce(
        acc[:, 1:2], amin[:, :], AX, ALU.min))
    t_m = emit_stream_maxmin_out(p, acc, m5, (tred,), pfx="m5o")
    return p.finalize([st_prev[0], st_prev[1], t_m])


def emit_down_scales(p, m2all, m3all, m4all, m5all, ps):
    c2, _ = emit_cell_max(p, m2all, NCORES, "m2")
    s2c, _, _ = emit_cell_scale(p, c2, "s2")
    c3, _ = emit_cell_max(p, m3all, NCORES, "m3")
    s3c, _, _ = emit_cell_scale(p, c3, "s3")
    c4raw, _ = emit_cell_max(p, m4all, NCORES, "m4")
    c34 = p.sbuf("c34", [1, 1], F32)
    p.op("vector", lambda v: v.tensor_scalar(
        c34[:, :], s3c[:, :], 2.0 ** -16, None, ALU.mult))
    m4v = p.sbuf("m4v", [1, 1], F32)
    p.op("vector", lambda v: v.tensor_tensor(m4v[:, :], c4raw[:, :], c34[:, :], ALU.mult))
    s4c, _, _ = emit_cell_scale(p, m4v, "s4")
    c42 = p.sbuf("c42", [1, 1], F32)
    p.op("vector", lambda v: v.tensor_tensor(c42[:, :], s4c[:, :], s2c[:, :], ALU.mult))
    c5raw, _ = emit_cell_max(p, m5all, NCORES, "m5")
    m5v = p.sbuf("m5v", [1, 1], F32)
    p.op("vector", lambda v: v.tensor_tensor(m5v[:, :], c5raw[:, :], c42[:, :], ALU.mult))
    s5c, s5r, _ = emit_cell_scale(p, m5v, "s5")
    a5p = p.sbuf("a5p", [1, 1], F32)
    t = p.op("vector", lambda v: v.tensor_tensor(a5p[:, :], c42[:, :], s5r[:, :], ALU.mult))
    a5col = p.sbuf("a5col", [128, 1], F32)
    s5col = p.sbuf("s5col", [128, 1], F32)
    t_sc = emit_bcast_cols(p, [a5p, s5c], [a5col, s5col], ps[6][:, 0:2], waits=(t,))
    return a5col, s5col, t_sc


def emit_down_stream(p, OP_dr, wq_dn, obf, otile, oint, a5col, s5col,
                     t_sc, t_w, evac_cb):
    ps = p.psum_pool
    QT = 512
    NQ = NTOK // QT
    rd_prev = [None, None]
    mm_all = {}
    evac_tags = {}
    qz_tags = {}
    last_evac = None

    def plan_qz(q):
        jq = q % 2
        csl = slice(q * QT, (q + 1) * QT)
        qz = None
        for i in range(HC):
            j = i % 2
            wv = []
            if rd_prev[j] is not None:
                wv.append(rd_prev[j])
            elif q == 0 and i < 2:
                wv.append(t_sc)
            if i == 0 and q >= 2:
                wv.append(mm_all[q - 2])
            t_in = p.dma("sync", otile[j], OP_dr[i * 128:(i + 1) * 128, csl],
                         waits=tuple(wv))
            rd_prev[j] = p.op("vector", lambda v, j=j: v.tensor_scalar(
                oint, otile[j], a5col[:, :], None, ALU.mult), waits=(t_in,))
            qz = p.op("vector", lambda v, i=i, jq=jq: v.tensor_scalar(
                obf[jq][:, i * QT:(i + 1) * QT], oint, s5col[:, :], None,
                ALU.mult))
        qz_tags[q] = qz

    def plan_mm(q, ot):
        jq = q % 2
        bank = ot % 4
        mml = None
        for i in range(HC):
            wsl = wq_dn[:, i * D + ot * 128: i * D + (ot + 1) * 128]
            xsl = obf[jq][:, i * QT:(i + 1) * QT]
            wv = []
            if i == 0:
                wv = [t_w, qz_tags[q]]
                pv = evac_tags.get(bank)
                if pv is not None:
                    wv.append(pv)
            mml = p.op("tensor", lambda t, wsl=wsl, xsl=xsl, i=i:
                       t.matmul(ps[bank][:, :], wsl, xsl,
                                start=(i == 0), stop=(i == HC - 1)),
                       waits=tuple(wv), tag=(i == HC - 1))
        return mml

    mm_tag = {}
    plan_qz(0)
    for q in range(NQ):
        for ot in range(DC):
            mm_tag[(q, ot)] = plan_mm(q, ot)
            if ot == 1 and q + 1 < NQ:
                plan_qz(q + 1)
            if ot >= 2:
                tg = evac_cb(ot - 2, q, ps[(ot - 2) % 4][:, :],
                             (mm_tag[(q, ot - 2)],))
                evac_tags[(ot - 2) % 4] = tg
                last_evac = tg
        for ot in (DC - 2, DC - 1):
            tg = evac_cb(ot, q, ps[ot % 4][:, :], (mm_tag[(q, ot)],))
            evac_tags[ot % 4] = tg
            last_evac = tg
        mm_all[q] = mm_tag[(q, DC - 1)]
    return last_evac


def emit_down_block(p, OP_dr, wdnN, wdnT, m_alls, evac_cb, pfx="dn"):
    ps = p.psum_pool
    m2all, m3all, m4all, m5all = m_alls
    a5col, s5col, t_sc = emit_down_scales(p, m2all, m3all, m4all, m5all, ps)
    wq_dn = p.sbuf(pfx + "_wq", [128, HC * D], BF16)
    t_w = emit_dequant(p, wdnN, wdnT, wq_dn, D, HID, (ps[4], ps[5]), pfx)
    obf = [p.sbuf(pfx + f"_obf{j}", [128, HC * 512], BF16) for j in range(2)]
    otile = [p.arena(512, F32) for j in range(2)]
    oint = p.arena(512, I32)
    return emit_down_stream(p, OP_dr, wq_dn, obf, otile, oint, a5col, s5col,
                            t_sc, t_w, evac_cb)


def emit_down_block_biga(p, OP_dr, wdnN, wdnT, m_alls, evac_cb, biga_view):
    ps = p.psum_pool
    m2all, m3all, m4all, m5all = m_alls
    a5col, s5col, t_sc = emit_down_scales(p, m2all, m3all, m4all, m5all, ps)
    wq_dn = p.dn_wq_view
    t_w = emit_dequant(p, wdnN, wdnT, wq_dn, D, HID, (ps[4], ps[5]), "dnA")
    half = HC * D // 2
    obf = [biga_view(half + j * (half // 2), HC * 512, BF16) for j in range(2)]
    otile = [p.arena(512, F32) for j in range(2)]
    oint = p.arena(512, I32)
    return emit_down_stream(p, OP_dr, wq_dn, obf, otile, oint, a5col, s5col,
                            t_sc, t_w, evac_cb)


def build_L9(name="L9"):
    p = Prog(name)
    OP_raw = p.dram_in("OP_raw", [HID, NTOK])
    xT = p.dram_in("xT", [D, NTOK])
    wdnN = p.dram_in("wdnN", [D, HID])
    wdnT = p.dram_in("wdnT", [HID, D])
    m_alls = [p.dram_in(nm, [1, NCORES]) for nm in ("m2all", "m3all", "m4all", "m5all")]
    yT = p.dram_out("yT", [D, NTOK])
    p.arena_reset()
    prep_dq_scr(p, D, HID)

    xtile = [p.sbuf(f"xt{j}", [128, 512], F32) for j in range(2)]
    ytile = [p.sbuf(f"yt{j}", [128, 512], F32) for j in range(2)]
    state = {"ld": {}, "st": {}}

    def evac_cb(ot, q, psum, waits):
        j = ot % 2
        ncol0 = q * 512
        wv = list(waits)
        t_x = p.dma("gpsimd", xtile[j][:, :],
                    xT[ot * 128:(ot + 1) * 128, ncol0:ncol0 + 512],
                    waits=tuple([state["st"][j]] if j in state["st"] else []))
        tv = p.op("vector", lambda v, j=j: v.scalar_tensor_tensor(
            ytile[j][:, :], psum, 0.5, xtile[j][:, :], ALU.mult, ALU.add),
            waits=tuple(wv) + (t_x,))
        state["ld"][j] = tv
        t_st = p.dma("scalar", yT[ot * 128:(ot + 1) * 128, ncol0:ncol0 + 512],
                     ytile[j][:, :], waits=(tv,))
        state["st"][j] = t_st
        return tv

    emit_down_block(p, OP_raw, wdnN, wdnT, m_alls, evac_cb)
    return p.finalize([state["st"][0], state["st"][1]])




def build_L5(name="L5"):
    p = Prog(name)
    OP_raw = p.dram_in("OP_raw", [HID, NTOK])
    xT = p.dram_in("xT", [D, NTOK])
    wdnN = p.dram_in("wdnN", [D, HID])
    wdnT = p.dram_in("wdnT", [HID, D])
    m_alls = [p.dram_in(nm, [1, NCORES]) for nm in ("m2all", "m3all", "m4all", "m5all")]
    wrT = p.dram_in("wrT", [D, D])
    wkT = p.dram_in("wkT", [D, D])
    wvT = p.dram_in("wvT", [D, D])
    wgT = p.dram_in("wgT", [D, D])
    woT = p.dram_in("woT", [D, D])
    ldT = p.dram_in("ldT", [D, 64])
    luT = p.dram_in("luT", [64, D])
    wb_t = p.dram_in("wb_t", [128, DC])
    u_t = p.dram_in("u_t", [128, DC])
    grep_dr = p.dram_in("grep", [128, 1024])
    brep_dr = p.dram_in("brep", [128, 1024])
    x2T = p.dram_out("x2T", [D, NTOK])
    m1p = p.dram_out("m1p", [1, 1])
    ab2 = p.dram_out("ab2", [2, NTOK])
    if DEBUG_L5:
        w_dbg = p.dram_out("w_dbg", [D, NTOK])
        A_dbg = p.dram_out("A_dbg", [D, NTOK])

    ps = p.psum_pool
    x1_dr = p.dram_tmp("x1_dr", [D, NTOK])
    rt_dr = p.dram_tmp("rt_dr", [D, NTOK], BF16)
    kt_dr = p.dram_tmp("kt_dr", [D, NTOK], BF16)
    kh_dr = p.dram_tmp("kh_dr", [D, NTOK], BF16)
    v_dr = p.dram_tmp("v_dr", [NTOK, D], BF16)
    g_dr = p.dram_tmp("g_dr", [NTOK, D], BF16)

    biga = p.sbuf("biga", [128, 24576], F32)

    def biga_view(off_f32, ncols, dtype=F32, rows=128):
        v = biga[0:rows, off_f32:off_f32 + (ncols * mybir.dt.size(dtype) + 3) // 4]
        if dtype != F32:
            v = v.bitcast(dtype)[0:rows, 0:ncols]
        return v

    bank_rd = {}

    def bank_guard(j):
        t = bank_rd.get(j)
        return (t,) if t is not None else ()

    p.arena_reset(16384)
    p.dn_wq_view = biga_view(0, HC * D, BF16)
    prep_dq_scr(p, D, HID)
    xtile = [p.arena(512, F32) for j in range(2)]
    ytile = [p.arena(512, F32) for j in range(2)]
    stA = {"st": {}, "rd": {}}

    def evac_x1(ot, q, psum, waits):
        j = ot % 2
        ncol0 = q * 512
        t_x = p.dma("gpsimd", xtile[j],
                    xT[ot * 128:(ot + 1) * 128, ncol0:ncol0 + 512],
                    waits=tuple([stA["st"][j]] if j in stA["st"] else []))
        tv = p.op("vector", lambda v, j=j: v.scalar_tensor_tensor(
            ytile[j], psum, 0.5, xtile[j], ALU.mult, ALU.add),
            waits=tuple(waits) + (t_x,))
        t_st = p.dma("scalar", x1_dr[ot * 128:(ot + 1) * 128, ncol0:ncol0 + 512],
                     ytile[j], waits=(tv,))
        stA["st"][j] = t_st
        return tv

    emit_down_block_biga(p, OP_raw, wdnN, wdnT, m_alls, evac_x1, biga_view)
    pbA = p.phase_barrier()

    p.arena_reset()
    arow1 = p.sbuf("arow1", [1, NTOK], F32)
    brow1 = p.sbuf("brow1", [1, NTOK], F32)
    mkb = mk_dram_streamer(p, x1_dr, first_waits=pbA)
    t_rows1 = emit_ln_rows(p, mkb, arow1, brow1, ps[0], ps[1], pfx="lnb")
    xb = biga_view(0, DC * NTOK, BF16)
    mkb2 = mk_dram_streamer(p, x1_dr, first_waits=pbA)
    t1s = p.arena(512, F32)
    t_xb = None
    prev = None
    for nt in range(NT):
        wv = (t_rows1,) if prev is None else (prev,)
        ta, tb = emit_ab_bcast(p, arow1, brow1, nt, ps[2], ps[3], waits=wv)
        for c in range(DC):
            xap, t_x = mkb2(nt, c)
            p.op("vector", lambda v, xap=xap: v.tensor_tensor(
                t1s, xap, ps[2][:, :], ALU.mult), waits=(ta, t_x))
            sl = slice(c * NTOK + nt * 512, c * NTOK + (nt + 1) * 512)
            prev = p.op("vector", lambda v, sl=sl: v.tensor_tensor(
                xb[:, sl], t1s, ps[3][:, :], ALU.add), waits=(tb,))
    t_xb = prev
    pbB = p.phase_barrier()

    p.arena_reset()
    identbf, tidb = p.const_sb("identbf", np.eye(128, dtype=BF))
    onescol_bf, t_oc = p.const_sb("onescol_bf", np.ones((128, 1), BF))
    epsc, t_eps = p.const_sb("eps_col", np.full((128, 1), EPS, np.float32))
    onec, t_one = p.const_sb("one_col", np.ones((128, 1), np.float32))
    zeros128, t_z = p.const_sb("zeros128", np.zeros((128, 128), np.float32))
    t1_sb = p.arena(NTOK, BF16, rows=64)
    Dall = p.sbuf("Dall", [128, 512], F32)
    ECall = p.sbuf("ECall", [128, 256], F32)
    wbcol = p.sbuf("wbcol", [128, DC], F32)
    ucol = p.sbuf("ucol", [128, DC], F32)
    t_wb = p.dma("sync", wbcol[:, :], wb_t[:, :], waits=pbB)
    t_uc = p.dma("sync", ucol[:, :], u_t[:, :], waits=pbB)

    ld_sb = p.sbuf("ld_sb", [128, 64], BF16)
    mmprev = None
    for c in range(DC):
        t_l = p.dma("gpsimd", ld_sb[:, :], ldT[c * 128:(c + 1) * 128, :],
                    waits=pbB if c == 0 else (mmprev,))
        for nt in range(NT):
            mmc = p.op("tensor", lambda t, c=c, nt=nt: t.matmul(
                ps[nt][0:64, :], ld_sb[:, :],
                xb[:, c * NTOK + nt * 512: c * NTOK + (nt + 1) * 512],
                start=(c == 0), stop=(c == DC - 1)),
                waits=(t_l, t_xb) + (bank_guard(nt) if c == 0 else ()),
                tag=(nt == NT - 1))
        mmprev = mmc
    for nt in range(NT):
        tg = p.op("scalar", lambda s, nt=nt: s.activation(
            out=t1_sb[:, nt * 512:(nt + 1) * 512], in_=ps[nt][0:64, :],
            func=AF.Tanh), waits=(mmprev,))
        bank_rd[nt] = tg

    A_f = biga_view(DC * NTOK // 2, NTOK, F32)
    eA = biga_view(DC * NTOK // 2 + NTOK, NTOK, F32)
    eN = eA
    r_t = p.arena(NTOK, BF16)
    k_t = p.arena(NTOK, BF16)
    stg2 = [p.arena(NTOK, BF16) for _ in range(2)]
    ruk = p.arena(NTOK, BF16)
    za = p.arena(512, F32)
    rr = p.arena(512, F32)
    wbuf = [p.sbuf(f"wrk_w{j}", [128, 128], BF16) for j in range(4)]
    lu_sb = p.sbuf("lu_sb", [64, 128], BF16)

    st_stage = {}
    wbuf_use = [None] * 4
    nload = [0]
    t_stt_prev = [None]
    truk_prev = [None]
    dmm_prev = [None]
    eAreg_rd = [None]
    afree = [None]
    st_dma = {}

    def plan_front(h2):
        hsl = slice(h2 * 128, (h2 + 1) * 128)
        for pi, (wT_d, dst) in enumerate(((wrT, r_t), (wkT, k_t))):
            mmprev = None
            for c in range(DC):
                j = nload[0] % 4
                nload[0] += 1
                t_w = p.dma("gpsimd", wbuf[j][:, :],
                            wT_d[c * 128:(c + 1) * 128, hsl],
                            waits=(wbuf_use[j],) if wbuf_use[j] is not None else ())
                for nt in range(NT):
                    mmc = p.op("tensor", lambda t, c=c, nt=nt, j=j: t.matmul(
                        ps[nt][:, :], wbuf[j][:, :],
                        xb[:, c * NTOK + nt * 512: c * NTOK + (nt + 1) * 512],
                        start=(c == 0), stop=(c == DC - 1)),
                        waits=(t_w, t_xb) + (bank_guard(nt) if c == 0 else ()),
                        tag=(nt == NT - 1))
                wbuf_use[j] = mmc
                mmprev = mmc
            for nt in range(NT):
                wv = [mmprev]
                if nt == 0 and truk_prev[0] is not None:
                    wv.append(truk_prev[0])
                tg = p.op("scalar", lambda s, nt=nt, dst=dst: s.copy(
                    out=dst[:, nt * 512:(nt + 1) * 512], in_=ps[nt][:, :]),
                    waits=tuple(wv))
                bank_rd[nt] = tg
        t_lu = p.dma("gpsimd", lu_sb[:, :], luT[:, hsl], waits=(p.last("tensor"),))
        for nt in range(NT):
            p.op("tensor", lambda t, nt=nt: t.matmul(
                ps[nt][:, :], lu_sb[:, :],
                t1_sb[:, nt * 512:(nt + 1) * 512], start=True, stop=True),
                waits=(t_lu,) + bank_guard(nt))
        t_mm_w = p.last("tensor")
        for nt in range(NT):
            sl = slice(nt * 512, (nt + 1) * 512)
            p.op("scalar", lambda s, nt=nt, h2=h2: s.activation(
                out=za, in_=ps[nt][:, :], func=AF.Abs, bias=wbcol[:, h2:h2 + 1]),
                waits=(t_mm_w, t_wb) +
                ((t_stt_prev[0],) if t_stt_prev[0] else ()))
            p.op("scalar", lambda s: s.activation(
                out=za, in_=za, func=AF.Exp, scale=-1.0))
            p.op("scalar", lambda s: s.activation(
                out=za, in_=za, func=AF.Ln, bias=onec[:, :]), waits=(t_one,))
            tr_ = p.op("scalar", lambda s, nt=nt, h2=h2: s.activation(
                out=rr, in_=ps[nt][:, :], func=AF.Relu, bias=wbcol[:, h2:h2 + 1]))
            wv = [tr_]
            if nt == 0 and afree[0] is not None:
                wv.append(afree[0])
            tg = p.op("vector", lambda v, sl=sl: v.scalar_tensor_tensor(
                A_f[:, sl], rr, -1.0, za, ALU.mult, ALU.subtract),
                waits=tuple(wv))
            t_stt_prev[0] = tg
            bank_rd[nt] = tg

    def plan_tail(h2):
        hsl = slice(h2 * 128, (h2 + 1) * 128)
        buf_r = stg2[h2 % 2]
        buf_k = stg2[(h2 + 1) % 2]
        for b2 in range(NB):
            for ch in range(NCH):
                sl = slice(b2 * T + ch * CHK, b2 * T + (ch + 1) * CHK)
                p.op("vector", lambda v, sl=sl: v.tensor_tensor_scan(
                    A_f[:, sl], A_f[:, sl], zeros128[:, :], 0.0, ALU.add,
                    ALU.add), waits=(t_z,))
        t_scan = p.last("vector")
        t_eA = p.op("scalar", lambda s: s.activation(out=eA, in_=A_f,
                                                     func=AF.Exp),
                    waits=(t_scan,) +
                    ((eAreg_rd[0],) if eAreg_rd[0] is not None else ()))
        eA3 = eA.rearrange("p (s c) -> p s c", c=CHK)
        t_EC = p.op("vector", lambda v, h2=h2: v.tensor_copy(
            ECall[:, h2 * 32:(h2 + 1) * 32],
            eA3[:, :, CHK - 1:CHK].rearrange("p s c -> p (s c)")), waits=(t_eA,))
        p.op("vector", lambda v, buf_r=buf_r: v.tensor_tensor(
            buf_r[:, 1:NTOK], r_t[:, 1:NTOK], eA[:, 0:NTOK - 1], ALU.mult),
            waits=(t_eA,) +
            ((st_dma[h2 % 2],) if h2 % 2 in st_dma else ()))
        r3 = r_t.rearrange("p (s c) -> p s c", c=CHK)
        s3v = buf_r.rearrange("p (s c) -> p s c", c=CHK)
        tfix = p.op("vector", lambda v, s3v=s3v, r3=r3: v.tensor_copy(
            s3v[:, :, 0:1], r3[:, :, 0:1]))
        d_rt = p.dma("sync", rt_dr[hsl, :], buf_r[:, :], waits=(tfix,))
        t_eN = p.op("scalar", lambda s: s.activation(out=eN, in_=A_f,
                                                     func=AF.Exp, scale=-1.0),
                    waits=(t_EC, tfix))
        afree[0] = t_eN
        tkt = p.op("vector", lambda v, buf_k=buf_k: v.tensor_tensor(
            buf_k[:, :], k_t[:, :], eN, ALU.mult),
            waits=(t_eN,) +
            ((st_dma[(h2 + 1) % 2],) if (h2 + 1) % 2 in st_dma else ()))
        d_kt = p.dma("sync", kt_dr[hsl, :], buf_k[:, :], waits=(tkt,))
        st_dma[(h2 + 1) % 2] = d_kt
        ec3 = ECall[:, h2 * 32:(h2 + 1) * 32].rearrange(
            "p (s o) -> p s o", o=1).broadcast_to((128, 32, CHK))
        eN3 = eN.rearrange("p (s c) -> p s c", c=CHK)
        p.op("vector", lambda v, eN3=eN3, ec3=ec3:
             v.tensor_tensor(eN3, eN3, ec3, ALU.mult))
        k3o = buf_r.rearrange("p (s c) -> p s c", c=CHK)
        tkh = p.op("vector", lambda v, k3o=k3o: v.tensor_tensor(
            k3o, k_t.rearrange("p (s c) -> p s c", c=CHK), eN3, ALU.mult),
            waits=(d_rt,))
        eAreg_rd[0] = tkh
        d_kh = p.dma("sync", kh_dr[hsl, :], buf_r[:, :], waits=(tkh,))
        st_dma[h2 % 2] = d_kh
        truk = p.op("vector", lambda v, h2=h2: v.scalar_tensor_tensor(
            ruk[:, :], r_t[:, :], ucol[:, h2:h2 + 1], k_t[:, :],
            ALU.mult, ALU.mult),
            waits=(t_uc,) + ((dmm_prev[0],) if dmm_prev[0] is not None else ()))
        truk_prev[0] = truk
        dmm = None
        for b2 in range(NB):
            for ch in range(NCH):
                sidx = b2 * NCH + ch
                nsl = slice(b2 * T + ch * CHK, b2 * T + (ch + 1) * CHK)
                for g in range(2):
                    rsl = slice(g * 64, (g + 1) * 64)
                    dmm = p.op("tensor", lambda t, rsl=rsl, nsl=nsl, g=g,
                               sidx=sidx:
                               t.matmul(ps[6][:, g * 32 + sidx: g * 32 + sidx + 1],
                                        ruk[rsl, nsl], onescol_bf[rsl, 0:1],
                                        start=True, stop=True),
                               waits=(truk, t_oc) +
                               (bank_guard(6) if sidx + g == 0 else ()),
                               tag=(g == 1 and sidx == NB * NCH - 1))
        dmm_prev[0] = dmm
        tg = p.op("vector", lambda v, h2=h2: v.tensor_copy(
            Dall[:, h2 * 64:(h2 + 1) * 64], ps[6][:, 0:64]), waits=(dmm,))
        bank_rd[6] = tg

    for h2 in range(DC):
        if h2 >= 1:
            plan_tail(h2 - 1)
        plan_front(h2)
    plan_tail(DC - 1)
    pbC = p.phase_barrier()

    p.arena_reset()
    wv_sb = p.arena(DC * D, BF16)
    wg_sb = p.arena(DC * D, BF16)
    wld = []
    for c in range(DC):
        wld.append(p.dma("gpsimd", wv_sb[:, c * D:(c + 1) * D],
                         wvT[c * 128:(c + 1) * 128, :], waits=pbC if c == 0 else ()))
        wld.append(p.dma("gpsimd", wg_sb[:, c * D:(c + 1) * D],
                         wgT[c * 128:(c + 1) * 128, :], waits=()))
    vg_stage = p.arena(D, BF16)
    vg_stage2 = p.arena(D, BF16)
    d_v = d_g = None
    for tt_ in range(NTOK // 128):
        ntc = tt_ % NT
        for c in range(DC):
            base = c * NTOK + tt_ * 128
            wv = (pbC + tuple(wld) + (t_xb,)) if (tt_ == 0 and c == 0) else \
                 (tuple(wld),) and ()
            p.op("tensor", lambda t, base=base, c=c: t.matmul(
                ps[0][:, :], xb[:, base:base + 128], wv_sb[:, c * D:c * D + 512],
                start=(c == 0), stop=(c == DC - 1)),
                waits=((bank_guard(0) + tuple(wld) + (t_xb,)) if c == 0 else ()),
                tag=False)
            p.op("tensor", lambda t, base=base, c=c: t.matmul(
                ps[1][:, :], xb[:, base:base + 128], wv_sb[:, c * D + 512:c * D + 1024],
                start=(c == 0), stop=(c == DC - 1)),
                waits=(bank_guard(1) if c == 0 else ()), tag=False)
            p.op("tensor", lambda t, base=base, c=c: t.matmul(
                ps[2][:, :], xb[:, base:base + 128], wg_sb[:, c * D:c * D + 512],
                start=(c == 0), stop=(c == DC - 1)),
                waits=(bank_guard(2) if c == 0 else ()), tag=False)
            p.op("tensor", lambda t, base=base, c=c: t.matmul(
                ps[3][:, :], xb[:, base:base + 128], wg_sb[:, c * D + 512:c * D + 1024],
                start=(c == 0), stop=(c == DC - 1)),
                waits=(bank_guard(3) if c == 0 else ()),
                tag=(c == DC - 1))
        t_mmd = p.last("tensor")
        tv1 = p.op("vector", lambda v: v.tensor_copy(vg_stage[:, 0:512], ps[0][:, :]),
                   waits=(t_mmd,) + ((d_v,) if d_v is not None else ()))
        tv2 = p.op("vector", lambda v: v.tensor_copy(vg_stage[:, 512:1024], ps[1][:, :]))
        bank_rd[0] = tv1
        bank_rd[1] = tv2
        d_v = p.dma("sync", v_dr[tt_ * 128:(tt_ + 1) * 128, :], vg_stage[:, :],
                    waits=(tv2,))
        tg1 = p.op("scalar", lambda s: s.activation(
            out=vg_stage2[:, 0:512], in_=ps[2][:, :], func=AF.Sigmoid),
            waits=(t_mmd,) + ((d_g,) if d_g is not None else ()))
        tg2 = p.op("scalar", lambda s: s.activation(
            out=vg_stage2[:, 512:1024], in_=ps[3][:, :], func=AF.Sigmoid))
        bank_rd[2] = tg1
        bank_rd[3] = tg2
        d_g = p.dma("sync", g_dr[tt_ * 128:(tt_ + 1) * 128, :], vg_stage2[:, :],
                    waits=(tg2,))
    pbD = p.phase_barrier()

    p.arena_reset()
    stens = [biga_view(DC * NTOK // 2 + 1024 * j, 1024, F32) for j in range(2)]
    rt_ch = [p.arena(1024, BF16) for _ in range(2)]
    kt_ch = [p.arena(1024, BF16) for _ in range(2)]
    kh_ch = [p.arena(1024, BF16) for _ in range(2)]
    v_ch = [p.arena(1024, BF16) for _ in range(2)]
    g_ch = [p.arena(1024, BF16) for _ in range(2)]
    out_chunk = p.arena(1024, F32)
    stens_bf = p.arena(512, BF16)
    sq_scr = p.arena(1024, F32)
    tscr = p.arena(1024, F32)
    ybf = p.arena(1024, BF16)
    khT_sb = p.sbuf("khT_sb", [128, 128], BF16)
    M_sbA = p.sbuf("M_sbA", [128, 128], BF16)
    M_sbB = p.sbuf("M_sbB", [128, 128], BF16)
    st_stats = p.sbuf("st_stats", [128, 64], F32)
    maskU, t_mu_ = p.const_sb("maskU", np.triu(np.ones((128, 128), np.float32), 1))
    grep_sb = p.arena(1024, F32)
    brep_sb = p.arena(1024, F32)
    t_gr = p.dma("sync", grep_sb[:, :], grep_dr[:, :], waits=pbD)
    t_br = p.dma("sync", brep_sb[:, :], brep_dr[:, :], waits=pbD)
    y_fm = biga_view(0, DC * NTOK, BF16)

    tm0 = p.op("vector", lambda v: v.memset(stens[0][:, :], 0.0), waits=pbD)
    ld_prev = {}
    y_done = []
    for ci in range(NCH):
        pin, pout = ci % 2, (ci + 1) % 2
        for b2 in range(NB):
            jj = (ci * NB + b2) % 2
            sidx = b2 * NCH + ci
            n0 = b2 * T + ci * CHK
            wvl = tuple(ld_prev.get(jj, ())) or (pbD if ci == 0 else ())
            lt = []
            for (dr, dstl, qn) in ((rt_dr, rt_ch, "sync"), (kt_dr, kt_ch, "gpsimd"),
                                   (kh_dr, kh_ch, "sync")):
                lt.append(p.dma(qn, dstl[jj],
                                dr[:, n0:n0 + CHK].rearrange("(c p) t -> p c t",
                                                             p=128),
                                waits=wvl))
            lt.append(p.dma("gpsimd", v_ch[jj], v_dr[n0:n0 + CHK, :], waits=wvl))
            lt.append(p.dma("sync", g_ch[jj], g_dr[n0:n0 + CHK, :], waits=wvl))
            lt = tuple(lt)
            tsb = p.op("vector", lambda v, pin=pin, b2=b2: v.tensor_copy(
                stens_bf[:, :], stens[pin][:, b2 * 512:(b2 + 1) * 512]),
                waits=(tm0,))
            ev_list = []
            for h2 in range(DC):
                hsl = slice(h2 * 128, (h2 + 1) * 128)
                ps4bf = ps[4][:, 0:64].bitcast(BF16)
                ttr_ = p.op("tensor", lambda t, jj=jj, hsl=hsl: t.transpose(
                    ps4bf[:, 0:128], kh_ch[jj][:, hsl], identbf),
                    waits=lt + (tidb,) + bank_guard(4))
                tev = p.op("vector", lambda v: v.tensor_copy(
                    khT_sb[:, :], ps4bf[:, 0:128]), waits=(ttr_,))
                bank_rd[4] = tev
                mmA = p.op("tensor", lambda t, jj=jj, hsl=hsl: t.matmul(
                    ps[0][:, 0:128], kt_ch[jj][0:64, hsl], rt_ch[jj][0:64, hsl],
                    start=True, stop=True), waits=lt + bank_guard(0))
                mmB = p.op("tensor", lambda t, jj=jj, hsl=hsl: t.matmul(
                    ps[3][:, 0:128], kt_ch[jj][64:128, hsl], rt_ch[jj][64:128, hsl],
                    start=True, stop=True), waits=lt + bank_guard(3))
                tMA = p.op("vector", lambda v: v.tensor_tensor(
                    M_sbA[:, :], ps[0][:, 0:128], maskU, ALU.mult),
                    waits=(mmA, t_mu_))
                bank_rd[0] = tMA
                tMB = p.op("vector", lambda v: v.tensor_tensor(
                    M_sbB[:, :], ps[3][:, 0:128], maskU, ALU.mult), waits=(mmB,))
                bank_rd[3] = tMB
                for g in range(2):
                    head = 2 * h2 + g
                    rsl = slice(g * 64, (g + 1) * 64)
                    opb = ps[1] if g == 0 else ps[2]
                    spsum = ps[4][0:64, 128:192] if g == 0 else ps[5][0:64, 0:64]
                    spbank = 4 if g == 0 else 5
                    scol = (b2 * DC + h2) * 64
                    Msb = M_sbA if g == 0 else M_sbB
                    tM = tMA if g == 0 else tMB
                    mm1 = p.op("tensor", lambda t, Msb=Msb, jj=jj, head=head,
                               opb=opb:
                               t.matmul(opb[:, 0:64], Msb[:, :],
                                        v_ch[jj][:, head * 64:(head + 1) * 64],
                                        start=True, stop=False),
                               waits=(tM,) + bank_guard(1 if g == 0 else 2),
                               tag=False)
                    mm2 = p.op("tensor", lambda t, jj=jj, rsl=rsl, hsl=hsl,
                               h2=h2, opb=opb:
                               t.matmul(opb[:, 0:64], rt_ch[jj][rsl, hsl],
                                        stens_bf[rsl, h2 * 64:(h2 + 1) * 64],
                                        start=False, stop=True),
                               waits=(tsb,))
                    mm3 = p.op("tensor", lambda t, jj=jj, rsl=rsl, head=head,
                               spsum=spsum:
                               t.matmul(spsum, khT_sb[:, rsl],
                                        v_ch[jj][:, head * 64:(head + 1) * 64],
                                        start=True, stop=True),
                               waits=(tev,) + bank_guard(spbank))
                    tse = p.op("vector", lambda v, rsl=rsl, scol=scol, pin=pin,
                               pout=pout, h2=h2, sidx=sidx, spsum=spsum:
                               v.scalar_tensor_tensor(
                                   stens[pout][rsl, scol:scol + 64],
                                   stens[pin][rsl, scol:scol + 64],
                                   ECall[rsl, h2 * 32 + sidx:h2 * 32 + sidx + 1],
                                   spsum, ALU.mult, ALU.add), waits=(mm3,))
                    bank_rd[spbank] = tse
                    toe = p.op("vector", lambda v, jj=jj, head=head, sidx=sidx,
                               opb=opb:
                               v.scalar_tensor_tensor(
                                   out_chunk[:, head * 64:(head + 1) * 64],
                                   v_ch[jj][:, head * 64:(head + 1) * 64],
                                   Dall[:, head * 32 + sidx:head * 32 + sidx + 1],
                                   opb[:, 0:64], ALU.mult, ALU.add),
                               waits=(mm2, mm1))
                    bank_rd[1 if g == 0 else 2] = toe
                    ev_list.append(toe)
            t_sq = p.op("scalar", lambda s: s.activation(
                out=sq_scr, in_=out_chunk, func=AF.Square), waits=(ev_list[-1],))
            p.op("vector", lambda v: v.tensor_reduce(
                st_stats[:, 0:16].rearrange("p (g b) -> p g b", b=1),
                out_chunk.rearrange("p (g b) -> p g b", b=64), AX, ALU.add))
            p.op("vector", lambda v: v.tensor_reduce(
                st_stats[:, 16:32].rearrange("p (g b) -> p g b", b=1),
                sq_scr.rearrange("p (g b) -> p g b", b=64), AX, ALU.add),
                waits=(t_sq,))
            p.op("vector", lambda v: v.tensor_scalar(
                st_stats[:, 0:16], st_stats[:, 0:16], 1.0 / 64, None, ALU.mult))
            p.op("vector", lambda v: v.tensor_tensor(
                st_stats[:, 32:48], st_stats[:, 0:16], st_stats[:, 0:16], ALU.mult))
            tv_ = p.op("vector", lambda v: v.scalar_tensor_tensor(
                st_stats[:, 16:32], st_stats[:, 16:32], 1.0 / 64,
                st_stats[:, 32:48], ALU.mult, ALU.subtract))
            p.op("scalar", lambda s: s.activation(
                out=st_stats[:, 16:32], in_=st_stats[:, 16:32], func=AF.Ln,
                bias=epsc[:, :]), waits=(tv_, t_eps))
            t_rs = p.op("scalar", lambda s: s.activation(
                out=st_stats[:, 16:32], in_=st_stats[:, 16:32], func=AF.Exp,
                scale=-0.5))
            mu_b = st_stats[:, 0:16].rearrange("p (g b) -> p g b", b=1) \
                .broadcast_to((128, 16, 64))
            rs_b = st_stats[:, 16:32].rearrange("p (g b) -> p g b", b=1) \
                .broadcast_to((128, 16, 64))
            oc3 = out_chunk.rearrange("p (g b) -> p g b", b=64)
            ts3 = tscr.rearrange("p (g b) -> p g b", b=64)
            p.op("vector", lambda v: v.tensor_tensor(ts3, oc3, mu_b, ALU.subtract),
                 waits=(t_rs,))
            p.op("vector", lambda v: v.tensor_tensor(ts3, ts3, rs_b, ALU.mult))
            p.op("vector", lambda v: v.tensor_tensor(tscr, tscr, grep_sb[:, :],
                                                     ALU.mult), waits=(t_gr,))
            p.op("vector", lambda v: v.tensor_tensor(tscr, tscr, brep_sb[:, :],
                                                     ALU.add), waits=(t_br,))
            t_y = p.op("vector", lambda v, jj=jj: v.tensor_tensor(
                ybf, tscr, g_ch[jj][:, :], ALU.mult))
            for c8 in range(DC):
                ps6bf = ps[6][:, 0:64].bitcast(BF16)
                ttb = p.op("tensor", lambda t, c8=c8: t.transpose(
                    ps6bf[:, 0:128], ybf[:, c8 * 128:(c8 + 1) * 128], identbf),
                    waits=(t_y, tidb) + bank_guard(6))
                tyc = p.op("vector", lambda v, c8=c8, n0=n0: v.tensor_copy(
                    y_fm[:, c8 * NTOK + n0: c8 * NTOK + n0 + CHK],
                    ps6bf[:, 0:128]), waits=(ttb,))
                bank_rd[6] = tyc
                y_done.append(tyc)
            ld_prev[jj] = (t_y, p.last("tensor"))
    pbE = p.phase_barrier()

    p.arena_reset()
    wo_sb = p.sbuf("wo_sb", [128, 128], BF16)
    x1t = [p.arena(512, F32) for j in range(2)]
    x2t = [p.arena(512, F32) for j in range(2)]
    stG = {"st": {}, "ld": {}}
    st_last = None
    mmprev = None
    for ot in range(DC):
        for c in range(DC):
            t_w = p.dma("gpsimd", wo_sb[:, :], woT[c * 128:(c + 1) * 128,
                                                   ot * 128:(ot + 1) * 128],
                        waits=(mmprev,) if (ot + c) else pbE)
            for nt in range(NT):
                mmc = p.op("tensor", lambda t, c=c, nt=nt: t.matmul(
                    ps[nt][:, :], wo_sb[:, :],
                    y_fm[:, c * NTOK + nt * 512: c * NTOK + (nt + 1) * 512],
                    start=(c == 0), stop=(c == DC - 1)),
                    waits=(t_w,) + (bank_guard(nt) if c == 0 else ()),
                    tag=(nt == NT - 1))
            mmprev = mmc
        t_mmo = mmprev
        for nt in range(NT):
            j = nt % 2
            nsl = slice(nt * 512, (nt + 1) * 512)
            t_x1 = p.dma("sync", x1t[j],
                         x1_dr[ot * 128:(ot + 1) * 128, nsl],
                         waits=tuple([stG["ld"][j]] if j in stG["ld"] else list(pbE)))
            tv = p.op("vector", lambda v, j=j, nt=nt: v.tensor_tensor(
                x2t[j], ps[nt][:, :], x1t[j], ALU.add),
                waits=(t_mmo, t_x1) + tuple([stG["st"][j]] if j in stG["st"] else []))
            bank_rd[nt] = tv
            stG["ld"][j] = tv
            st_last = p.dma("scalar", x2T[ot * 128:(ot + 1) * 128, nsl],
                            x2t[j], waits=(tv,))
            stG["st"][j] = st_last
    pbG = p.phase_barrier()
    arow2, brow2 = arow1, brow1
    mk1 = mk_dram_streamer(p, x2T, first_waits=pbG)
    t_rows2 = emit_ln_rows(p, mk1, arow2, brow2, ps[0], ps[1], pfx="lng")
    t_ab1 = p.dma("sync", ab2[0:1, :], arow2[:, :], waits=(t_rows2,))
    t_ab2 = p.dma("sync", ab2[1:2, :], brow2[:, :], waits=(t_rows2,))
    mk2 = mk_dram_streamer(p, x2T, first_waits=pbG)
    t_m1p = emit_lnmax(p, mk2, arow2, brow2, m1p, ps, t_rows2, pfx="lmg")
    return p.finalize([st_last, t_ab1, t_ab2, t_m1p])



def _run(nc, in_maps):
    return run_bass_kernel_spmd(nc, in_maps, core_ids=list(range(NCORES))).results


def _cat_scalars(rs, key):
    return np.concatenate([r[key].reshape(1, 1) for r in rs], axis=1)


_PROGS = {}


def _prog(name, builder):
    if name not in _PROGS:
        _PROGS[name] = builder()
    return _PROGS[name]


def kernel(**inputs):
    x = np.asarray(inputs["L0_attn_x"], np.float32)
    w_r = np.asarray(inputs["L0_attn_w_r"], np.float32)
    w_k = np.asarray(inputs["L0_attn_w_k"], np.float32)
    w_v = np.asarray(inputs["L0_attn_w_v"], np.float32)
    w_o = np.asarray(inputs["L0_attn_w_o"], np.float32)
    w_g = np.asarray(inputs["L0_attn_w_g"], np.float32)
    w_base = np.asarray(inputs["L0_attn_w_base"], np.float32)
    w_ld = np.asarray(inputs["L0_attn_w_lora_down"], np.float32)
    w_lu = np.asarray(inputs["L0_attn_w_lora_up"], np.float32)
    u = np.asarray(inputs["L0_attn_u"], np.float32)
    ln_g = np.asarray(inputs["L0_attn_ln_gamma"], np.float32)
    ln_b = np.asarray(inputs["L0_attn_ln_beta"], np.float32)
    w_gate = np.asarray(inputs["L0_ffn_w_gate"], np.float32)
    w_up = np.asarray(inputs["L0_ffn_w_up"], np.float32)
    w_down = np.asarray(inputs["L0_ffn_w_down"], np.float32)

    C = np.ascontiguousarray
    xTs = [C(x[2 * c:2 * c + 2].transpose(2, 0, 1).reshape(D, NTOK))
           for c in range(NCORES)]
    wupT = C(w_up.T)
    wgateT = C(w_gate.T)
    wdnT = C(w_down.T)
    attn_common = {
        "wdnN": w_down, "wdnT": wdnT,
        "wrT": C(w_r.T), "wkT": C(w_k.T), "wvT": C(w_v.T), "wgT": C(w_g.T),
        "woT": C(w_o.T), "ldT": C(w_ld.T), "luT": C(w_lu.T),
        "wb_t": C(w_base.reshape(DC, 128).T), "u_t": C(u.reshape(DC, 128).T),
        "grep": C(np.tile(ln_g, (128, NHEAD))), "brep": C(np.tile(ln_b, (128, NHEAD))),
    }

    def ffn_half(xT_list, ab_list, m1all):
        nc2 = _prog("L2", build_L2)
        r2 = _run(nc2, [{"xT": xT_list[c], "ab": ab_list[c], "m1all": m1all,
                         "wupN": w_up, "wupT": wupT,
                         "wgateN": w_gate, "wgateT": wgateT}
                        for c in range(NCORES)])
        m2all = C(np.concatenate([r["m23"][:, 0:1] for r in r2], axis=1))
        m3all = C(np.concatenate([r["m23"][:, 1:2] for r in r2], axis=1))
        nc3 = _prog("L3", build_L3)
        r3 = _run(nc3, [{"gate_raw": r2[c]["gate_raw"], "m3all": m3all}
                        for c in range(NCORES)])
        m4all = _cat_scalars(r3, "m4")
        nc4 = _prog("L4", build_L4)
        r4 = _run(nc4, [{"GR_raw": r3[c]["GR_raw"], "up_raw": r2[c]["up_raw"],
                         "m2all": m2all, "m3all": m3all, "m4all": m4all}
                        for c in range(NCORES)])
        m5all = _cat_scalars(r4, "m5")
        op_list = [r4[c]["OP_raw"] for c in range(NCORES)]
        return op_list, m2all, m3all, m4all, m5all

    nc1 = _prog("L1", build_L1)
    r1 = _run(nc1, [{"xT": xTs[c]} for c in range(NCORES)])
    m1all = _cat_scalars(r1, "m1")
    ab1 = [r1[c]["ab"] for c in range(NCORES)]

    op1, m2a, m3a, m4a, m5a = ffn_half(xTs, ab1, m1all)

    nc5 = _prog("L5", build_L5)
    r5 = _run(nc5, [{**attn_common, "OP_raw": op1[c], "xT": xTs[c],
                     "m2all": m2a, "m3all": m3a, "m4all": m4a, "m5all": m5a}
                    for c in range(NCORES)])
    x2s = [r5[c]["x2T"] for c in range(NCORES)]
    m1pall = _cat_scalars(r5, "m1p")
    ab2 = [r5[c]["ab2"] for c in range(NCORES)]

    op2, m2b, m3b, m4b, m5b = ffn_half(x2s, ab2, m1pall)

    nc9 = _prog("L9", build_L9)
    r9 = _run(nc9, [{"OP_raw": op2[c], "xT": x2s[c], "wdnN": w_down,
                     "wdnT": wdnT, "m2all": m2b, "m3all": m3b,
                     "m4all": m4b, "m5all": m5b} for c in range(NCORES)])

    y = np.empty((16, T, D), np.float32)
    for c in range(NCORES):
        y[2 * c:2 * c + 2] = r9[c]["yT"].reshape(D, NB, T).transpose(1, 2, 0)
    return y


# version 08
# speedup vs baseline: 1.0124x; 1.0124x over previous
import numpy as np
import ml_dtypes
import concourse.bass as bass
import concourse.mybir as mybir
from concourse.bass_utils import run_bass_kernel_spmd

F32 = mybir.dt.float32
I32 = mybir.dt.int32
BF16 = mybir.dt.bfloat16
ALU = mybir.AluOpType
AF = mybir.ActivationFunctionType
AX = mybir.AxisListType.X
BF = ml_dtypes.bfloat16

NCORES = 8
NB = 2
T = 2048
NTOK = NB * T
D = 1024
DC = D // 128
HID = 3072
HC = HID // 128
NT = NTOK // 512
QMAX = 32767.0
EPS = 1e-5
NHEAD = 16
DH = 64
CHK = 128
NCH = T // CHK

CHAINED = ("vector", "scalar", "gpsimd")
DEBUG_L5 = False


class Prog:

    ENGS = ("sync", "tensor", "vector", "scalar", "gpsimd")

    def __init__(self, name):
        self.name = name
        self.nc = bass.Bass("TRN2", target_bir_lowering=False, debug=False)
        self.thunks = {e: [] for e in self.ENGS}
        self.tick = {e: 0 for e in self.ENGS}
        self.NLANES = 2
        self.dval = {(e, l): 0 for e in self.ENGS for l in range(self.NLANES)}
        self.dcnt = {e: 0 for e in self.ENGS}
        self.cache = {}
        self.psum_pool = [self.nc.alloc_psum_tensor(f"psb{j}", [128, 512], F32).ap()
                          for j in range(8)]
        self._arena = None
        self._arena_off = 0

    def arena_reset(self, ncols=14336):
        if self._arena is None:
            self._arena = self.sbuf("arena", [128, ncols], F32)
            self._arena_cols = ncols
        self._arena_off = 0

    def arena(self, ncols, dtype=F32, rows=128):
        fcols = (ncols * mybir.dt.size(dtype) + 3) // 4
        off = self._arena_off
        assert off + fcols <= self._arena_cols, (off, fcols, self._arena_cols)
        self._arena_off += fcols
        view = self._arena[0:rows, off:off + fcols]
        if dtype != F32:
            view = view.bitcast(dtype)[0:rows, 0:ncols]
        return view

    def last(self, eng):
        return ("e", eng, self.tick[eng])

    def dlast(self, queue):
        lane = (self.dcnt[queue] - 1) % self.NLANES
        return ("d", (queue, lane), self.dval[(queue, lane)])

    def phase_barrier(self):
        tags = [("e", e, self.tick[e]) for e in self.ENGS if self.tick[e] > 0]
        tags += [("d", k, v) for k, v in self.dval.items() if v > 0]
        return tuple(tags)

    def dram_in(self, name, shape, dtype=F32):
        return self.nc.dram_tensor(name, list(shape), dtype, kind="ExternalInput").ap()

    def dram_out(self, name, shape, dtype=F32):
        return self.nc.dram_tensor(name, list(shape), dtype, kind="ExternalOutput").ap()

    def dram_tmp(self, name, shape, dtype=F32):
        return self.nc.dram_tensor(name, list(shape), dtype, kind="Internal").ap()

    def sbuf(self, name, shape, dtype=F32):
        return self.nc.alloc_sbuf_tensor(name, list(shape), dtype).ap()

    def const_sb(self, name, arr, queue="gpsimd"):
        if name not in self.cache:
            arr = np.ascontiguousarray(arr)
            dt = {np.dtype(np.float32): F32, np.dtype(BF): BF16,
                  np.dtype(np.int32): I32}[arr.dtype]
            dr = self.nc.inline_tensor(arr, name).ap()
            sb = self.sbuf(name + "_sb", list(arr.shape), dt)
            tag = self.dma(queue, sb[:, :], dr[:, :])
            self.cache[name] = (sb[:, :], tag)
        return self.cache[name]

    def op(self, eng, fn, waits=(), tag=True):
        assert tag or eng == "tensor"
        if tag:
            self.tick[eng] += 1
        tick = self.tick[eng]
        chain = eng in CHAINED and tick > 1
        waits = tuple(waits)

        def thunk(engine, waits=waits, tick=tick, fn=fn, chain=chain, eng=eng,
                  tag=tag):
            if chain and tag:
                engine.wait_ge(self.sems[eng], tick - 1)
            for kind, src, val in waits:
                if kind == "e" and src == eng and val >= tick:
                    raise RuntimeError(f"self-deadlock {eng} {val}>={tick}")
                sem = self.sems[src] if kind == "e" else self.dsems[src]
                engine.wait_ge(sem, val)
            inst = fn(engine)
            if tag:
                inst.then_inc(self.sems[eng], 1)

        self.thunks[eng].append(thunk)
        return ("e", eng, tick)

    def dma(self, queue, out, in_, waits=()):
        lane = self.dcnt[queue] % self.NLANES
        self.dcnt[queue] += 1
        self.dval[(queue, lane)] += 16
        val = self.dval[(queue, lane)]
        waits = tuple(waits)

        def thunk(engine, waits=waits, out=out, in_=in_, queue=queue,
                  lane=lane, val=val):
            if val > 16:
                engine.wait_ge(self.dsems[(queue, lane)], val - 16)
            for kind, src, v in waits:
                sem = self.sems[src] if kind == "e" else self.dsems[src]
                engine.wait_ge(sem, v)
            engine.dma_start(out=out, in_=in_).then_inc(
                self.dsems[(queue, lane)], 16)

        self.thunks[queue].append(thunk)
        return ("d", (queue, lane), val)

    def finalize(self, final_waits):
        nc = self.nc
        import contextlib
        with contextlib.ExitStack() as stack:
            self.sems = {e: stack.enter_context(nc.semaphore(f"tk_{e}"))
                         for e in self.ENGS}
            self.dsems = {(e, l): stack.enter_context(nc.semaphore(f"dm_{e}{l}"))
                          for e in self.ENGS for l in range(self.NLANES)}
            block = stack.enter_context(nc.Block())

            def final_thunk(engine):
                for kind, src, val in final_waits:
                    sem = self.sems[src] if kind == "e" else self.dsems[src]
                    engine.wait_ge(sem, val)

            self.thunks["sync"].append(final_thunk)
            for eng_name, runner in (("sync", block.sync), ("tensor", block.tensor),
                                     ("vector", block.vector), ("scalar", block.scalar),
                                     ("gpsimd", block.gpsimd)):
                thunks = self.thunks[eng_name]
                if not thunks:
                    continue

                def body(engine, thunks=thunks):
                    for t in thunks:
                        t(engine)

                runner(body)
        return nc



def emit_scale(p, mall_dr, ncols, scol, rcol, ps, qmax=QMAX, floor=1e-12, pfx="sc"):
    mrow = p.sbuf(pfx + "_mrow", [1, ncols], F32)
    t_in = p.dma("sync", mrow[:, :], mall_dr[:, :])
    cell = p.sbuf(pfx + "_cell", [1, 1], F32)
    rcell = p.sbuf(pfx + "_rcell", [1, 1], F32)
    p.op("vector", lambda v: v.tensor_reduce(cell[:, :], mrow[:, :], AX, ALU.max),
         waits=(t_in,))
    t2 = p.op("vector", lambda v: v.tensor_scalar(
        cell[:, :], cell[:, :], 1.0 / qmax, floor, ALU.mult, ALU.max))
    t3 = p.op("vector", lambda v: v.reciprocal(rcell[:, :], cell[:, :]))
    onesr, t_o = p.const_sb("ones_1x128_f32", np.ones((1, 128), np.float32))
    tm1 = p.op("tensor", lambda t: t.matmul(ps[:, 0:1], onesr, cell[:, :],
                                            start=True, stop=True),
               waits=(t2, t3, t_o))
    tm2 = p.op("tensor", lambda t: t.matmul(ps[:, 1:2], onesr, rcell[:, :],
                                            start=True, stop=True))
    p.op("vector", lambda v: v.tensor_copy(scol[:, :], ps[:, 0:1]), waits=(tm1,))
    t5 = p.op("vector", lambda v: v.tensor_copy(rcol[:, :], ps[:, 1:2]), waits=(tm2,))
    return t5


def emit_absmax_cols_out(p, cols_sb, ntens, ps, out_dr, waits=(), pfx="amx"):
    cmb = p.sbuf(pfx + "_cmb", [128, ntens], F32)
    tprev = None
    for k in range(ntens):
        tprev = p.op("vector", lambda v, k=k: v.scalar_tensor_tensor(
            cmb[:, k:k + 1], cols_sb[:, 2 * k + 1:2 * k + 2], -1.0,
            cols_sb[:, 2 * k:2 * k + 1], ALU.mult, ALU.max),
            waits=waits if k == 0 else ())
    identf, tid = p.const_sb("ident128_f32", np.eye(128, dtype=np.float32))
    tt = p.op("tensor", lambda t: t.transpose(ps[0:ntens, 0:128], cmb[:, :],
                                              identf), waits=(tprev, tid))
    mcol = p.sbuf(pfx + "_mcol", [ntens, 1], F32)
    tr = p.op("vector", lambda v: v.tensor_reduce(
        mcol[:, :], ps[0:ntens, 0:128], AX, ALU.max), waits=(tt,))
    return p.dma("sync", out_dr[:, :], mcol[:, :], waits=(tr,))


def emit_ln_rows(p, xtile_dma, arow, brow, ps_mu, ps_m2, waits=(), pfx="ln"):
    onescol, t_oc = p.const_sb("onescol_bf", np.ones((128, 1), BF))
    epsc, t_eps = p.const_sb("eps_col", np.full((128, 1), EPS, np.float32))
    xhat = p.arena(512, BF16)
    sq = p.arena(512, BF16)
    mrow = p.arena(512, F32, rows=1)
    vrow = p.arena(512, F32, rows=1)
    prev_mm = ()
    t6 = None
    for nt in range(NT):
        for c in range(DC):
            xt, t_x = xtile_dma(nt, c, waits if (nt == 0 and c == 0) else ())
            tv = p.op("vector", lambda v, xt=xt: v.tensor_copy(xhat, xt),
                      waits=(t_x,) + prev_mm)
            ts_ = p.op("scalar", lambda s: s.activation(
                out=sq, in_=xhat, func=AF.Square),
                waits=(tv,) + prev_mm)
            wv1 = (tv, t_oc) if t6 is None else (tv, t_oc, t6)
            tm1 = p.op("tensor", lambda t, c=c: t.matmul(
                ps_mu[0:1, :], onescol, xhat,
                start=(c == 0), stop=(c == DC - 1)), waits=wv1)
            tm2 = p.op("tensor", lambda t, c=c: t.matmul(
                ps_m2[0:1, :], onescol, sq,
                start=(c == 0), stop=(c == DC - 1)), waits=(ts_,))
            prev_mm = (tm1, tm2)
        sl = slice(nt * 512, (nt + 1) * 512)
        p.op("vector", lambda v: v.tensor_scalar(
            mrow, ps_mu[0:1, :], 1.0 / D, None, ALU.mult), waits=prev_mm)
        p.op("vector", lambda v: v.tensor_tensor(
            vrow, mrow, mrow, ALU.mult))
        tv3 = p.op("vector", lambda v: v.scalar_tensor_tensor(
            vrow, ps_m2[0:1, :], 1.0 / D, vrow, ALU.mult, ALU.subtract))
        t4 = p.op("scalar", lambda s: s.activation(
            out=vrow, in_=vrow, func=AF.Ln, bias=epsc[0:1, :]),
            waits=(tv3, t_eps))
        t5 = p.op("scalar", lambda s, sl=sl: s.activation(
            out=arow[:, sl], in_=vrow, func=AF.Exp, scale=-0.5))
        t6 = p.op("vector", lambda v, sl=sl: v.scalar_tensor_tensor(
            brow[:, sl], mrow, -1.0, arow[:, sl], ALU.mult, ALU.mult),
            waits=(t5,))
    return t6


def emit_ab_bcast(p, arow, brow, nt, ps_a, ps_b, waits=()):
    onesr, t_o = p.const_sb("ones_1x128_f32", np.ones((1, 128), np.float32))
    sl = slice(nt * 512, (nt + 1) * 512)
    t1 = p.op("tensor", lambda t: t.matmul(ps_a[:, :], onesr, arow[:, sl],
                                           start=True, stop=True),
              waits=tuple(waits) + (t_o,))
    t2 = p.op("tensor", lambda t: t.matmul(ps_b[:, :], onesr, brow[:, sl],
                                           start=True, stop=True), waits=tuple(waits))
    return t1, t2



def emit_lnmax(p, mk_xtile, arow, brow, m_out, ps, t_rows, pfx="lm"):
    t1s = p.arena(512, F32)
    xns = p.arena(512, F32)
    junk = p.arena(512, BF16)
    NA = NT * DC
    amax = p.sbuf(pfx + "_amax", [128, NA], F32)
    amin = p.sbuf(pfx + "_amin", [128, NA], F32)
    prev = None
    for nt in range(NT):
        wv = (t_rows,) if prev is None else (prev,)
        ta, tb = emit_ab_bcast(p, arow, brow, nt, ps[2], ps[3], waits=wv)
        for c in range(DC):
            k = nt * DC + c
            xap, t_x = mk_xtile(nt, c)
            p.op("vector", lambda v, xap=xap: v.tensor_tensor(
                t1s, xap, ps[2][:, :], ALU.mult), waits=(ta, t_x))
            p.op("vector", lambda v: v.tensor_tensor(
                xns, t1s, ps[3][:, :], ALU.add), waits=(tb,))
            p.op("vector", lambda v, k=k: v.tensor_scalar(
                junk, xns, 1.0, None, ALU.mult, ALU.max,
                accum_out=amax[:, k:k + 1]))
            prev = p.op("vector", lambda v, k=k: v.tensor_scalar(
                junk, xns, 1.0, None, ALU.mult, ALU.min,
                accum_out=amin[:, k:k + 1]))
    acc2 = p.sbuf(pfx + "_acc2", [128, 2], F32)
    p.op("vector", lambda v: v.tensor_reduce(
        acc2[:, 0:1], amax[:, :], AX, ALU.max), waits=(prev,))
    tcp = p.op("vector", lambda v: v.tensor_reduce(
        acc2[:, 1:2], amin[:, :], AX, ALU.min))
    return emit_absmax_cols_out(p, acc2, 1, ps[4][:, 0:128], m_out, waits=(tcp,),
                                pfx=pfx + "_amx")


def mk_dram_streamer(p, src_dr, queue="sync", first_waits=()):
    bufs = [p.arena(512, F32) for j in range(2)]
    state = {"j": 0, "rd": [None, None]}

    def mk_xtile(nt, c, wv=()):
        j = state["j"]
        state["j"] = 1 - j
        wl = list(wv)
        if state["rd"][j] is None:
            wl += list(first_waits)
        else:
            wl.append(state["rd"][j])
        tag = p.dma(queue, bufs[j][:, :],
                    src_dr[c * 128:(c + 1) * 128, nt * 512:(nt + 1) * 512],
                    waits=tuple(wl))
        state["rd"][j] = ("e", "vector", p.tick["vector"] + 1)
        return bufs[j][:, :], tag

    return mk_xtile


def build_L1():
    p = Prog("L1")
    xT = p.dram_in("xT", [D, NTOK])
    m1 = p.dram_out("m1", [1, 1])
    ab = p.dram_out("ab", [2, NTOK])
    p.arena_reset()
    arow = p.sbuf("arow", [1, NTOK], F32)
    brow = p.sbuf("brow", [1, NTOK], F32)
    ps = p.psum_pool
    mk1 = mk_dram_streamer(p, xT)
    t_rows = emit_ln_rows(p, mk1, arow, brow, ps[0], ps[1])
    t_ab1 = p.dma("sync", ab[0:1, :], arow[:, :], waits=(t_rows,))
    t_ab2 = p.dma("sync", ab[1:2, :], brow[:, :], waits=(t_rows,))
    mk2 = mk_dram_streamer(p, xT)
    t_m1 = emit_lnmax(p, mk2, arow, brow, m1, ps, t_rows)
    return p.finalize([t_ab1, t_ab2, t_m1])



def emit_dequant(p, wnat, wT, wq_sb, nrows, nin, ps_t, pfx, waits=()):
    NBLK = nin // 32
    wnat_sb, bm, sc, wqn, qq, identbf, tid = p.dq_scr
    ntile = nrows // 128
    prev_tq = None
    prev_tr = {}
    for ot in range(ntile):
        wv = list(waits)
        if prev_tq is not None:
            wv.append(prev_tq)
        tl = p.dma("sync", wnat_sb, wnat[ot * 128:(ot + 1) * 128, :],
                   waits=tuple(wv))
        p.op("vector", lambda v: v.tensor_reduce(
            bm.rearrange("p (g b) -> p g b", b=1),
            wnat_sb.rearrange("p (g b) -> p g b", b=32),
            AX, ALU.max, apply_absolute_value=True), waits=(tl,))
        p.op("vector", lambda v: v.tensor_scalar(
            sc[:, 0:NBLK], bm, 1.0 / 7.0, 1e-12, ALU.mult, ALU.max))
        p.op("vector", lambda v: v.reciprocal(sc[:, NBLK:2 * NBLK], sc[:, 0:NBLK]))
        rb = sc[:, NBLK:2 * NBLK].rearrange("p (g b) -> p g b", b=1) \
            .broadcast_to((128, NBLK, 32))
        sb_ = sc[:, 0:NBLK].rearrange("p (g b) -> p g b", b=1) \
            .broadcast_to((128, NBLK, 32))
        w3 = wnat_sb.rearrange("p (g b) -> p g b", b=32)
        p.op("vector", lambda v: v.tensor_tensor(
            wnat_sb.rearrange("p (g b) -> p g b", b=32), w3, rb, ALU.mult))
        prev_tq = p.op("vector", lambda v: v.tensor_scalar(
            qq, wnat_sb, 1.0, None, ALU.mult))
        tw = p.op("vector", lambda v: v.tensor_tensor(
            wqn.rearrange("p (g b) -> p g b", b=32),
            qq.rearrange("p (g b) -> p g b", b=32), sb_, ALU.mult))
        for c in range(nin // 128):
            bank = ps_t[c % 2][:, 0:64].bitcast(BF16)[:, 0:128]
            wvt = (tw, tid) + (prev_tr.get(c % 2, ()) or ())
            tt = p.op("tensor", lambda t, bank=bank, c=c: t.transpose(
                bank, wqn[:, c * 128:(c + 1) * 128], identbf), waits=wvt)
            te = p.op("vector", lambda v, bank=bank, c=c, ot=ot: v.tensor_copy(
                wq_sb[:, c * nrows + ot * 128: c * nrows + (ot + 1) * 128],
                bank), waits=(tt,))
            prev_tr[c % 2] = (te,)
    return te


def prep_dq_scr(p, nrows, nin):
    NBLK = nin // 32
    identbf, tid = p.const_sb("identbf", np.eye(128, dtype=BF))
    p.dq_scr = (p.arena(nin, F32), p.arena(NBLK, F32), p.arena(2 * NBLK, F32),
                p.arena(nin, BF16), p.arena(nin, I32), identbf, tid)


def emit_xq(p, xT, ab, m1all, xq, ps, pfx=""):
    s1col = p.sbuf(pfx + "s1col", [128, 1], F32)
    a1col = p.sbuf(pfx + "a1col", [128, 1], F32)
    t_s1 = emit_scale(p, m1all, NCORES, s1col, a1col, ps[4][:, 0:2], pfx=pfx + "s1")
    arow = p.sbuf(pfx + "arow", [1, NTOK], F32)
    brow = p.sbuf(pfx + "brow", [1, NTOK], F32)
    tab1 = p.dma("sync", arow[:, :], ab[0:1, :])
    tab2 = p.dma("sync", brow[:, :], ab[1:2, :])
    xtile = [p.arena(512, F32) for _ in range(2)]
    t1s = p.arena(512, F32)
    xns = p.arena(512, F32)
    qi = p.arena(512, I32)
    prev = None
    prev_rd = [None, None]
    for nt in range(NT):
        wv = [tab1, tab2] if prev is None else [prev]
        ta, tb = emit_ab_bcast(p, arow, brow, nt, ps[2], ps[3], waits=tuple(wv))
        for c in range(DC):
            j = c % 2
            wvx = [] if prev_rd[j] is None else [prev_rd[j]]
            t_x = p.dma("sync", xtile[j],
                        xT[c * 128:(c + 1) * 128, nt * 512:(nt + 1) * 512],
                        waits=tuple(wvx))
            prev_rd[j] = p.op("vector", lambda v, j=j: v.tensor_tensor(
                t1s, xtile[j], ps[2][:, :], ALU.mult), waits=(ta, t_x))
            p.op("vector", lambda v: v.tensor_tensor(
                xns, t1s, ps[3][:, :], ALU.add), waits=(tb,))
            p.op("vector", lambda v: v.tensor_scalar(
                qi, xns, a1col[:, :], None, ALU.mult), waits=(t_s1,))
            sl = slice(c * NTOK + nt * 512, c * NTOK + (nt + 1) * 512)
            prev = p.op("vector", lambda v, sl=sl: v.tensor_scalar(
                xq[:, sl], qi, s1col[:, :], None, ALU.mult))
    return prev, s1col, a1col


def build_L2(name="L2"):
    p = Prog(name)
    xT = p.dram_in("xT", [D, NTOK])
    ab = p.dram_in("ab", [2, NTOK])
    m1all = p.dram_in("m1all", [1, NCORES])
    wupN = p.dram_in("wupN", [HID, D])
    wupT = p.dram_in("wupT", [D, HID])
    wgateN = p.dram_in("wgateN", [HID, D])
    wgateT = p.dram_in("wgateT", [D, HID])
    up_raw = p.dram_out("up_raw", [HID, NTOK])
    gate_raw = p.dram_out("gate_raw", [HID, NTOK])
    m23 = p.dram_out("m23", [1, 2])

    ps = p.psum_pool
    p.arena_reset(12800)
    xq = p.sbuf("xq", [128, DC * NTOK], BF16)
    t_xq, _, _ = emit_xq(p, xT, ab, m1all, xq, ps)

    prep_dq_scr(p, HID, D)

    wq = p.sbuf("wq", [128, DC * HID], BF16)
    out_sb = p.sbuf("out_sb", [128, 2 * 512], F32)
    junk = p.sbuf("junkbf", [128, 512], BF16)
    NACC = HC * NT
    amax = [p.sbuf(f"amax{w}", [128, NACC], F32) for w in range(2)]
    amin = [p.sbuf(f"amin{w}", [128, NACC], F32) for w in range(2)]

    store_last = None
    mm_waits_extra = []
    slot_store = [None, None]
    evac_last = None
    for wi, (wN, wTd, raw_dr) in enumerate(((wupN, wupT, up_raw),
                                            (wgateN, wgateT, gate_raw))):
        t_w = emit_dequant(p, wN, wTd, wq, HID, D, (ps[4], ps[5]), f"dq{wi}",
                           waits=tuple(mm_waits_extra))
        evac_prev = [None] * NT
        mm_all_last = None
        for ot in range(HC):
            mm_last = [None] * NT
            for c in range(DC):
                wsl = wq[:, c * HID + ot * 128: c * HID + (ot + 1) * 128]
                for nt in range(NT):
                    wv = []
                    if c == 0:
                        wv = [t_w, t_xq]
                        if evac_prev[nt] is not None:
                            wv.append(evac_prev[nt])
                    xsl = xq[:, c * NTOK + nt * 512: c * NTOK + (nt + 1) * 512]
                    mm_last[nt] = p.op(
                        "tensor", lambda t, wsl=wsl, xsl=xsl, c=c, nt=nt:
                        t.matmul(ps[nt][:, :], wsl, xsl,
                                 start=(c == 0), stop=(c == DC - 1)),
                        waits=tuple(wv), tag=(c == DC - 1))
            mm_all_last = mm_last[-1]
            for nt in range(NT):
                k = ot * NT + nt
                slot = nt % 2
                osl = slice(slot * 512, (slot + 1) * 512)
                wv = [mm_last[nt]]
                if slot_store[slot] is not None:
                    wv.append(slot_store[slot])
                p.op("vector", lambda v, osl=osl, nt=nt, k=k, wi=wi: v.tensor_scalar(
                    out_sb[:, osl], ps[nt][:, :], 1.0, None,
                    ALU.mult, ALU.max, accum_out=amax[wi][:, k:k + 1]),
                    waits=tuple(wv))
                evac_prev[nt] = p.op(
                    "vector", lambda v, osl=osl, k=k, wi=wi: v.tensor_scalar(
                        junk[:, :], out_sb[:, osl], 1.0, None,
                        ALU.mult, ALU.min, accum_out=amin[wi][:, k:k + 1]))
                evac_last = evac_prev[nt]
                slot_store[slot] = p.dma(
                    "scalar", raw_dr[ot * 128:(ot + 1) * 128, nt * 512:(nt + 1) * 512],
                    out_sb[:, osl], waits=(evac_prev[nt],))
        store_last = slot_store[(NT - 1) % 2]
        mm_waits_extra = [mm_all_last]
    acc = p.sbuf("acc", [128, 4], F32)
    tred = None
    for wi in range(2):
        p.op("vector", lambda v, wi=wi: v.tensor_reduce(
            acc[:, 2 * wi:2 * wi + 1], amax[wi][:, :], AX, ALU.max),
            waits=(evac_last,) if wi == 0 else ())
        tred = p.op("vector", lambda v, wi=wi: v.tensor_reduce(
            acc[:, 2 * wi + 1:2 * wi + 2], amin[wi][:, :], AX, ALU.min))
    t_m = emit_absmax_cols_out(p, acc[:, :], 2, ps[5][:, 0:128], m23,
                               waits=(tred,))
    return p.finalize([store_last, t_m])




def emit_cell_max(p, mall_dr, ncols, pfx, waits=()):
    mrow = p.sbuf(pfx + "_mrow", [1, ncols], F32)
    t_in = p.dma("sync", mrow[:, :], mall_dr[:, :], waits=waits)
    cell = p.sbuf(pfx + "_cell", [1, 1], F32)
    t = p.op("vector", lambda v: v.tensor_reduce(cell[:, :], mrow[:, :], AX, ALU.max),
             waits=(t_in,))
    return cell, t


def emit_cell_scale(p, cell_in, pfx, qmax=QMAX, floor=1e-12):
    sc = p.sbuf(pfx + "_s", [1, 1], F32)
    rc = p.sbuf(pfx + "_r", [1, 1], F32)
    p.op("vector", lambda v: v.tensor_scalar(
        sc[:, :], cell_in[:, :], 1.0 / qmax, floor, ALU.mult, ALU.max))
    t = p.op("vector", lambda v: v.reciprocal(rc[:, :], sc[:, :]))
    return sc, rc, t


def emit_bcast_cols(p, cells, cols, ps, waits=()):
    onesr, t_o = p.const_sb("ones_1x128_f32", np.ones((1, 128), np.float32))
    last = None
    for j, (cell, col) in enumerate(zip(cells, cols)):
        tm = p.op("tensor", lambda t, j=j, cell=cell: t.matmul(
            ps[:, j:j + 1], onesr, cell[:, :], start=True, stop=True),
            waits=tuple(waits) + (t_o,))
        last = p.op("vector", lambda v, j=j, col=col: v.tensor_copy(
            col[:, :], ps[:, j:j + 1]), waits=(tm,))
    return last



def emit_stream_maxmin_out(p, acc, out_dr, waits, pfx="sm"):
    return emit_absmax_cols_out(p, acc, 1, p.psum_pool[5][:, 0:128], out_dr,
                                waits=waits, pfx=pfx)


def build_L3(name="L3"):
    p = Prog(name)
    gate_raw = p.dram_in("gate_raw", [HID, NTOK])
    m3all = p.dram_in("m3all", [1, NCORES])
    GR_raw = p.dram_out("GR_raw", [HID, NTOK])
    m4 = p.dram_out("m4", [1, 1])
    ps = p.psum_pool
    p.arena_reset()

    c3, t0 = emit_cell_max(p, m3all, NCORES, "m3")
    s3c, a3c, t1 = emit_cell_scale(p, c3, "s3")
    s3col = p.sbuf("s3col", [128, 1], F32)
    a3col = p.sbuf("a3col", [128, 1], F32)
    t_sc = emit_bcast_cols(p, [s3c, a3c], [s3col, a3col], ps[6][:, 0:2],
                           waits=(t1,))

    gin = [p.sbuf(f"gin{j}", [128, NTOK], F32) for j in range(2)]
    gout = [p.sbuf(f"gout{j}", [128, NTOK], F32) for j in range(2)]
    G = p.sbuf("G", [128, NTOK], I32)
    sg = p.sbuf("sg", [128, NTOK], F32)
    R = p.sbuf("R", [128, NTOK], I32)
    junk = p.sbuf("junk", [128, NTOK], BF16)
    amax = p.sbuf("amax", [128, HC], F32)
    amin = p.sbuf("amin", [128, HC], F32)

    rd_prev = [None, None]
    st_prev = [None, None]
    last_v = None
    for ht in range(HC):
        j = ht % 2
        wv = [t_sc] if rd_prev[j] is None else [rd_prev[j]]
        t_in = p.dma("sync", gin[j][:, :], gate_raw[ht * 128:(ht + 1) * 128, :],
                     waits=tuple(wv))
        tG = p.op("vector", lambda v, j=j: v.tensor_scalar(
            G[:, :], gin[j][:, :], a3col[:, :], None, ALU.mult), waits=(t_in,))
        rd_prev[j] = tG
        tS = p.op("scalar", lambda s: s.activation(
            out=sg[:, :], in_=G[:, :], func=AF.Sigmoid, scale=s3col[:, :]),
            waits=(tG,))
        tR = p.op("vector", lambda v: v.tensor_scalar(
            R[:, :], sg[:, :], 65536.0, 65535.0, ALU.mult, ALU.min), waits=(tS,))
        wv2 = [tR]
        if st_prev[j] is not None:
            wv2.append(st_prev[j])
        tGR = p.op("vector", lambda v, j=j: v.tensor_tensor(
            gout[j][:, :], G[:, :], R[:, :], ALU.mult), waits=tuple(wv2))
        p.op("vector", lambda v, j=j, ht=ht: v.tensor_scalar(
            junk[:, :], gout[j][:, :], 1.0, None, ALU.mult, ALU.max,
            accum_out=amax[:, ht:ht + 1]))
        last_v = p.op("vector", lambda v, j=j, ht=ht: v.tensor_scalar(
            junk[:, :], gout[j][:, :], 1.0, None, ALU.mult, ALU.min,
            accum_out=amin[:, ht:ht + 1]))
        st_prev[j] = p.dma("scalar", GR_raw[ht * 128:(ht + 1) * 128, :],
                           gout[j][:, :], waits=(tGR,))
    acc = p.sbuf("acc", [128, 2], F32)
    p.op("vector", lambda v: v.tensor_reduce(
        acc[:, 0:1], amax[:, :], AX, ALU.max), waits=(last_v,))
    tred = p.op("vector", lambda v: v.tensor_reduce(
        acc[:, 1:2], amin[:, :], AX, ALU.min))
    t_m = emit_stream_maxmin_out(p, acc, m4, (tred,), pfx="m4o")
    return p.finalize([st_prev[0], st_prev[1], t_m])


def build_L4(name="L4"):
    p = Prog(name)
    GR_raw = p.dram_in("GR_raw", [HID, NTOK])
    up_raw = p.dram_in("up_raw", [HID, NTOK])
    m2all = p.dram_in("m2all", [1, NCORES])
    m3all = p.dram_in("m3all", [1, NCORES])
    m4all = p.dram_in("m4all", [1, NCORES])
    OP_raw = p.dram_out("OP_raw", [HID, NTOK])
    m5 = p.dram_out("m5", [1, 1])
    ps = p.psum_pool
    p.arena_reset()

    c2, _ = emit_cell_max(p, m2all, NCORES, "m2")
    s2c, a2c, _ = emit_cell_scale(p, c2, "s2")
    c3, _ = emit_cell_max(p, m3all, NCORES, "m3")
    s3c, _, _ = emit_cell_scale(p, c3, "s3")
    c4raw, _ = emit_cell_max(p, m4all, NCORES, "m4")
    c34 = p.sbuf("c34", [1, 1], F32)
    p.op("vector", lambda v: v.tensor_scalar(
        c34[:, :], s3c[:, :], 2.0 ** -16, None, ALU.mult))
    m4v = p.sbuf("m4v", [1, 1], F32)
    p.op("vector", lambda v: v.tensor_tensor(m4v[:, :], c4raw[:, :], c34[:, :], ALU.mult))
    s4c, s4r, _ = emit_cell_scale(p, m4v, "s4")
    a4p = p.sbuf("a4p", [1, 1], F32)
    t_ap = p.op("vector", lambda v: v.tensor_tensor(
        a4p[:, :], c34[:, :], s4r[:, :], ALU.mult))
    a4col = p.sbuf("a4col", [128, 1], F32)
    a2col = p.sbuf("a2col", [128, 1], F32)
    t_sc = emit_bcast_cols(p, [a4p, a2c], [a4col, a2col], ps[6][:, 0:2],
                           waits=(t_ap,))

    gin = [p.sbuf(f"gin{j}", [128, NTOK], F32) for j in range(2)]
    uin = [p.sbuf(f"uin{j}", [128, NTOK], F32) for j in range(2)]
    oout = [p.sbuf(f"oout{j}", [128, NTOK], F32) for j in range(2)]
    G2 = p.sbuf("G2", [128, NTOK], I32)
    U = p.sbuf("U", [128, NTOK], I32)
    junk = p.sbuf("junk", [128, NTOK], BF16)
    amax = p.sbuf("amax", [128, HC], F32)
    amin = p.sbuf("amin", [128, HC], F32)

    rd_prev = [None, None]
    st_prev = [None, None]
    last_v = None
    for ht in range(HC):
        j = ht % 2
        wv = [t_sc] if rd_prev[j] is None else [rd_prev[j]]
        t_g = p.dma("sync", gin[j][:, :], GR_raw[ht * 128:(ht + 1) * 128, :],
                    waits=tuple(wv))
        t_u = p.dma("gpsimd", uin[j][:, :], up_raw[ht * 128:(ht + 1) * 128, :],
                    waits=tuple(wv))
        tG = p.op("vector", lambda v, j=j: v.tensor_scalar(
            G2[:, :], gin[j][:, :], a4col[:, :], None, ALU.mult), waits=(t_g,))
        tU = p.op("vector", lambda v, j=j: v.tensor_scalar(
            U[:, :], uin[j][:, :], a2col[:, :], None, ALU.mult), waits=(t_u,))
        rd_prev[j] = tU
        wv2 = [tU]
        if st_prev[j] is not None:
            wv2.append(st_prev[j])
        tOP = p.op("vector", lambda v, j=j: v.tensor_tensor(
            oout[j][:, :], G2[:, :], U[:, :], ALU.mult), waits=tuple(wv2))
        p.op("vector", lambda v, j=j, ht=ht: v.tensor_scalar(
            junk[:, :], oout[j][:, :], 1.0, None, ALU.mult, ALU.max,
            accum_out=amax[:, ht:ht + 1]))
        last_v = p.op("vector", lambda v, j=j, ht=ht: v.tensor_scalar(
            junk[:, :], oout[j][:, :], 1.0, None, ALU.mult, ALU.min,
            accum_out=amin[:, ht:ht + 1]))
        st_prev[j] = p.dma("scalar", OP_raw[ht * 128:(ht + 1) * 128, :],
                           oout[j][:, :], waits=(tOP,))
    acc = p.sbuf("acc", [128, 2], F32)
    p.op("vector", lambda v: v.tensor_reduce(
        acc[:, 0:1], amax[:, :], AX, ALU.max), waits=(last_v,))
    tred = p.op("vector", lambda v: v.tensor_reduce(
        acc[:, 1:2], amin[:, :], AX, ALU.min))
    t_m = emit_stream_maxmin_out(p, acc, m5, (tred,), pfx="m5o")
    return p.finalize([st_prev[0], st_prev[1], t_m])


def emit_down_scales(p, m2all, m3all, m4all, m5all, ps):
    c2, _ = emit_cell_max(p, m2all, NCORES, "m2")
    s2c, _, _ = emit_cell_scale(p, c2, "s2")
    c3, _ = emit_cell_max(p, m3all, NCORES, "m3")
    s3c, _, _ = emit_cell_scale(p, c3, "s3")
    c4raw, _ = emit_cell_max(p, m4all, NCORES, "m4")
    c34 = p.sbuf("c34", [1, 1], F32)
    p.op("vector", lambda v: v.tensor_scalar(
        c34[:, :], s3c[:, :], 2.0 ** -16, None, ALU.mult))
    m4v = p.sbuf("m4v", [1, 1], F32)
    p.op("vector", lambda v: v.tensor_tensor(m4v[:, :], c4raw[:, :], c34[:, :], ALU.mult))
    s4c, _, _ = emit_cell_scale(p, m4v, "s4")
    c42 = p.sbuf("c42", [1, 1], F32)
    p.op("vector", lambda v: v.tensor_tensor(c42[:, :], s4c[:, :], s2c[:, :], ALU.mult))
    c5raw, _ = emit_cell_max(p, m5all, NCORES, "m5")
    m5v = p.sbuf("m5v", [1, 1], F32)
    p.op("vector", lambda v: v.tensor_tensor(m5v[:, :], c5raw[:, :], c42[:, :], ALU.mult))
    s5c, s5r, _ = emit_cell_scale(p, m5v, "s5")
    a5p = p.sbuf("a5p", [1, 1], F32)
    t = p.op("vector", lambda v: v.tensor_tensor(a5p[:, :], c42[:, :], s5r[:, :], ALU.mult))
    a5col = p.sbuf("a5col", [128, 1], F32)
    s5col = p.sbuf("s5col", [128, 1], F32)
    t_sc = emit_bcast_cols(p, [a5p, s5c], [a5col, s5col], ps[6][:, 0:2], waits=(t,))
    return a5col, s5col, t_sc


def emit_down_stream(p, OP_dr, wq_dn, obf, otile, oint, a5col, s5col,
                     t_sc, t_w, evac_cb):
    ps = p.psum_pool
    QT = 512
    NQ = NTOK // QT
    rd_prev = [None, None]
    mm_all = {}
    evac_tags = {}
    qz_tags = {}
    last_evac = None

    def plan_qz(q):
        jq = q % 2
        csl = slice(q * QT, (q + 1) * QT)
        qz = None
        for i in range(HC):
            j = i % 2
            wv = []
            if rd_prev[j] is not None:
                wv.append(rd_prev[j])
            elif q == 0 and i < 2:
                wv.append(t_sc)
            if i == 0 and q >= 2:
                wv.append(mm_all[q - 2])
            t_in = p.dma("sync", otile[j], OP_dr[i * 128:(i + 1) * 128, csl],
                         waits=tuple(wv))
            rd_prev[j] = p.op("vector", lambda v, j=j: v.tensor_scalar(
                oint, otile[j], a5col[:, :], None, ALU.mult), waits=(t_in,))
            qz = p.op("vector", lambda v, i=i, jq=jq: v.tensor_scalar(
                obf[jq][:, i * QT:(i + 1) * QT], oint, s5col[:, :], None,
                ALU.mult))
        qz_tags[q] = qz

    def plan_mm(q, ot):
        jq = q % 2
        bank = ot % 4
        mml = None
        for i in range(HC):
            wsl = wq_dn[:, i * D + ot * 128: i * D + (ot + 1) * 128]
            xsl = obf[jq][:, i * QT:(i + 1) * QT]
            wv = []
            if i == 0:
                wv = [t_w, qz_tags[q]]
                pv = evac_tags.get(bank)
                if pv is not None:
                    wv.append(pv)
            mml = p.op("tensor", lambda t, wsl=wsl, xsl=xsl, i=i:
                       t.matmul(ps[bank][:, :], wsl, xsl,
                                start=(i == 0), stop=(i == HC - 1)),
                       waits=tuple(wv), tag=(i == HC - 1))
        return mml

    mm_tag = {}
    plan_qz(0)
    for q in range(NQ):
        for ot in range(DC):
            mm_tag[(q, ot)] = plan_mm(q, ot)
            if ot == 1 and q + 1 < NQ:
                plan_qz(q + 1)
            if ot >= 2:
                tg = evac_cb(ot - 2, q, ps[(ot - 2) % 4][:, :],
                             (mm_tag[(q, ot - 2)],))
                evac_tags[(ot - 2) % 4] = tg
                last_evac = tg
        for ot in (DC - 2, DC - 1):
            tg = evac_cb(ot, q, ps[ot % 4][:, :], (mm_tag[(q, ot)],))
            evac_tags[ot % 4] = tg
            last_evac = tg
        mm_all[q] = mm_tag[(q, DC - 1)]
    return last_evac


def emit_down_block(p, OP_dr, wdnN, wdnT, m_alls, evac_cb, pfx="dn"):
    ps = p.psum_pool
    m2all, m3all, m4all, m5all = m_alls
    a5col, s5col, t_sc = emit_down_scales(p, m2all, m3all, m4all, m5all, ps)
    wq_dn = p.sbuf(pfx + "_wq", [128, HC * D], BF16)
    t_w = emit_dequant(p, wdnN, wdnT, wq_dn, D, HID, (ps[4], ps[5]), pfx)
    obf = [p.sbuf(pfx + f"_obf{j}", [128, HC * 512], BF16) for j in range(2)]
    otile = [p.arena(512, F32) for j in range(2)]
    oint = p.arena(512, I32)
    return emit_down_stream(p, OP_dr, wq_dn, obf, otile, oint, a5col, s5col,
                            t_sc, t_w, evac_cb)


def emit_down_block_biga(p, OP_dr, wdnN, wdnT, m_alls, evac_cb, biga_view):
    ps = p.psum_pool
    m2all, m3all, m4all, m5all = m_alls
    a5col, s5col, t_sc = emit_down_scales(p, m2all, m3all, m4all, m5all, ps)
    wq_dn = p.dn_wq_view
    t_w = emit_dequant(p, wdnN, wdnT, wq_dn, D, HID, (ps[4], ps[5]), "dnA")
    half = HC * D // 2
    obf = [biga_view(half + j * (half // 2), HC * 512, BF16) for j in range(2)]
    otile = [p.arena(512, F32) for j in range(2)]
    oint = p.arena(512, I32)
    return emit_down_stream(p, OP_dr, wq_dn, obf, otile, oint, a5col, s5col,
                            t_sc, t_w, evac_cb)


def build_L9(name="L9"):
    p = Prog(name)
    OP_raw = p.dram_in("OP_raw", [HID, NTOK])
    xT = p.dram_in("xT", [D, NTOK])
    wdnN = p.dram_in("wdnN", [D, HID])
    wdnT = p.dram_in("wdnT", [HID, D])
    m_alls = [p.dram_in(nm, [1, NCORES]) for nm in ("m2all", "m3all", "m4all", "m5all")]
    yT = p.dram_out("yT", [D, NTOK])
    p.arena_reset()
    prep_dq_scr(p, D, HID)

    xtile = [p.sbuf(f"xt{j}", [128, 512], F32) for j in range(2)]
    ytile = [p.sbuf(f"yt{j}", [128, 512], F32) for j in range(2)]
    state = {"ld": {}, "st": {}}

    def evac_cb(ot, q, psum, waits):
        j = ot % 2
        ncol0 = q * 512
        wv = list(waits)
        t_x = p.dma("gpsimd", xtile[j][:, :],
                    xT[ot * 128:(ot + 1) * 128, ncol0:ncol0 + 512],
                    waits=tuple([state["st"][j]] if j in state["st"] else []))
        tv = p.op("vector", lambda v, j=j: v.scalar_tensor_tensor(
            ytile[j][:, :], psum, 0.5, xtile[j][:, :], ALU.mult, ALU.add),
            waits=tuple(wv) + (t_x,))
        state["ld"][j] = tv
        t_st = p.dma("scalar", yT[ot * 128:(ot + 1) * 128, ncol0:ncol0 + 512],
                     ytile[j][:, :], waits=(tv,))
        state["st"][j] = t_st
        return tv

    emit_down_block(p, OP_raw, wdnN, wdnT, m_alls, evac_cb)
    return p.finalize([state["st"][0], state["st"][1]])




def build_L5(name="L5"):
    p = Prog(name)
    OP_raw = p.dram_in("OP_raw", [HID, NTOK])
    xT = p.dram_in("xT", [D, NTOK])
    wdnN = p.dram_in("wdnN", [D, HID])
    wdnT = p.dram_in("wdnT", [HID, D])
    m_alls = [p.dram_in(nm, [1, NCORES]) for nm in ("m2all", "m3all", "m4all", "m5all")]
    wrT = p.dram_in("wrT", [D, D])
    wkT = p.dram_in("wkT", [D, D])
    wvT = p.dram_in("wvT", [D, D])
    wgT = p.dram_in("wgT", [D, D])
    woT = p.dram_in("woT", [D, D])
    ldT = p.dram_in("ldT", [D, 64])
    luT = p.dram_in("luT", [64, D])
    wb_t = p.dram_in("wb_t", [128, DC])
    u_t = p.dram_in("u_t", [128, DC])
    grep_dr = p.dram_in("grep", [128, 1024])
    brep_dr = p.dram_in("brep", [128, 1024])
    x2T = p.dram_out("x2T", [D, NTOK])
    m1p = p.dram_out("m1p", [1, 1])
    ab2 = p.dram_out("ab2", [2, NTOK])
    if DEBUG_L5:
        w_dbg = p.dram_out("w_dbg", [D, NTOK])
        A_dbg = p.dram_out("A_dbg", [D, NTOK])

    ps = p.psum_pool
    x1_dr = p.dram_tmp("x1_dr", [D, NTOK])
    rt_dr = p.dram_tmp("rt_dr", [D, NTOK], BF16)
    kt_dr = p.dram_tmp("kt_dr", [D, NTOK], BF16)
    kh_dr = p.dram_tmp("kh_dr", [D, NTOK], BF16)
    v_dr = p.dram_tmp("v_dr", [NTOK, D], BF16)
    g_dr = p.dram_tmp("g_dr", [NTOK, D], BF16)

    biga = p.sbuf("biga", [128, 24576], F32)

    def biga_view(off_f32, ncols, dtype=F32, rows=128):
        v = biga[0:rows, off_f32:off_f32 + (ncols * mybir.dt.size(dtype) + 3) // 4]
        if dtype != F32:
            v = v.bitcast(dtype)[0:rows, 0:ncols]
        return v

    bank_rd = {}

    def bank_guard(j):
        t = bank_rd.get(j)
        return (t,) if t is not None else ()

    p.arena_reset(16384)
    p.dn_wq_view = biga_view(0, HC * D, BF16)
    prep_dq_scr(p, D, HID)
    xtile = [p.arena(512, F32) for j in range(2)]
    ytile = [p.arena(512, F32) for j in range(2)]
    stA = {"st": {}, "rd": {}}

    def evac_x1(ot, q, psum, waits):
        j = ot % 2
        ncol0 = q * 512
        t_x = p.dma("gpsimd", xtile[j],
                    xT[ot * 128:(ot + 1) * 128, ncol0:ncol0 + 512],
                    waits=tuple([stA["st"][j]] if j in stA["st"] else []))
        tv = p.op("vector", lambda v, j=j: v.scalar_tensor_tensor(
            ytile[j], psum, 0.5, xtile[j], ALU.mult, ALU.add),
            waits=tuple(waits) + (t_x,))
        t_st = p.dma("scalar", x1_dr[ot * 128:(ot + 1) * 128, ncol0:ncol0 + 512],
                     ytile[j], waits=(tv,))
        stA["st"][j] = t_st
        return tv

    emit_down_block_biga(p, OP_raw, wdnN, wdnT, m_alls, evac_x1, biga_view)
    pbA = p.phase_barrier()

    p.arena_reset()
    arow1 = p.sbuf("arow1", [1, NTOK], F32)
    brow1 = p.sbuf("brow1", [1, NTOK], F32)
    mkb = mk_dram_streamer(p, x1_dr, first_waits=pbA)
    t_rows1 = emit_ln_rows(p, mkb, arow1, brow1, ps[0], ps[1], pfx="lnb")
    xb = biga_view(0, DC * NTOK, BF16)
    mkb2 = mk_dram_streamer(p, x1_dr, first_waits=pbA)
    t1s = p.arena(512, F32)
    t_xb = None
    prev = None
    for nt in range(NT):
        wv = (t_rows1,) if prev is None else (prev,)
        ta, tb = emit_ab_bcast(p, arow1, brow1, nt, ps[2], ps[3], waits=wv)
        for c in range(DC):
            xap, t_x = mkb2(nt, c)
            p.op("vector", lambda v, xap=xap: v.tensor_tensor(
                t1s, xap, ps[2][:, :], ALU.mult), waits=(ta, t_x))
            sl = slice(c * NTOK + nt * 512, c * NTOK + (nt + 1) * 512)
            prev = p.op("vector", lambda v, sl=sl: v.tensor_tensor(
                xb[:, sl], t1s, ps[3][:, :], ALU.add), waits=(tb,))
    t_xb = prev
    pbB = p.phase_barrier()

    p.arena_reset()
    identbf, tidb = p.const_sb("identbf", np.eye(128, dtype=BF))
    onescol_bf, t_oc = p.const_sb("onescol_bf", np.ones((128, 1), BF))
    epsc, t_eps = p.const_sb("eps_col", np.full((128, 1), EPS, np.float32))
    onec, t_one = p.const_sb("one_col", np.ones((128, 1), np.float32))
    zeros128, t_z = p.const_sb("zeros128", np.zeros((128, 128), np.float32))
    t1_sb = p.arena(NTOK, BF16, rows=64)
    Dall = p.sbuf("Dall", [128, 512], F32)
    ECall = p.sbuf("ECall", [128, 256], F32)
    wbcol = p.sbuf("wbcol", [128, DC], F32)
    ucol = p.sbuf("ucol", [128, DC], F32)
    t_wb = p.dma("sync", wbcol[:, :], wb_t[:, :], waits=pbB)
    t_uc = p.dma("sync", ucol[:, :], u_t[:, :], waits=pbB)

    ld_sb = p.sbuf("ld_sb", [128, 64], BF16)
    mmprev = None
    for c in range(DC):
        t_l = p.dma("gpsimd", ld_sb[:, :], ldT[c * 128:(c + 1) * 128, :],
                    waits=pbB if c == 0 else (mmprev,))
        for nt in range(NT):
            mmc = p.op("tensor", lambda t, c=c, nt=nt: t.matmul(
                ps[nt][0:64, :], ld_sb[:, :],
                xb[:, c * NTOK + nt * 512: c * NTOK + (nt + 1) * 512],
                start=(c == 0), stop=(c == DC - 1)),
                waits=(t_l, t_xb) + (bank_guard(nt) if c == 0 else ()),
                tag=(nt == NT - 1))
        mmprev = mmc
    for nt in range(NT):
        tg = p.op("scalar", lambda s, nt=nt: s.activation(
            out=t1_sb[:, nt * 512:(nt + 1) * 512], in_=ps[nt][0:64, :],
            func=AF.Tanh), waits=(mmprev,))
        bank_rd[nt] = tg

    A_f = biga_view(DC * NTOK // 2, NTOK, F32)
    eA = biga_view(DC * NTOK // 2 + NTOK, NTOK, F32)
    eN = eA
    r_t = p.arena(NTOK, BF16)
    k_t = p.arena(NTOK, BF16)
    stg2 = [p.arena(NTOK, BF16) for _ in range(2)]
    ruk = p.arena(NTOK, BF16)
    za = p.arena(512, F32)
    rr = p.arena(512, F32)
    wbuf = [p.sbuf(f"wrk_w{j}", [128, 128], BF16) for j in range(4)]
    lu_sb = p.sbuf("lu_sb", [64, 128], BF16)

    st_stage = {}
    wbuf_use = [None] * 4
    nload = [0]
    t_stt_prev = [None]
    truk_prev = [None]
    dmm_prev = [None]
    eAreg_rd = [None]
    afree = [None]
    st_dma = {}

    def plan_front(h2):
        hsl = slice(h2 * 128, (h2 + 1) * 128)
        for pi, (wT_d, dst) in enumerate(((wrT, r_t), (wkT, k_t))):
            mmprev = None
            for c in range(DC):
                j = nload[0] % 4
                nload[0] += 1
                t_w = p.dma("gpsimd", wbuf[j][:, :],
                            wT_d[c * 128:(c + 1) * 128, hsl],
                            waits=(wbuf_use[j],) if wbuf_use[j] is not None else ())
                for nt in range(NT):
                    wv = (t_w,) if nt == 0 else ()
                    if h2 == 0 and pi == 0 and c == 0 and nt == 0:
                        wv = wv + (t_xb,)
                    if c == 0:
                        wv = wv + bank_guard(nt)
                    mmc = p.op("tensor", lambda t, c=c, nt=nt, j=j: t.matmul(
                        ps[nt][:, :], wbuf[j][:, :],
                        xb[:, c * NTOK + nt * 512: c * NTOK + (nt + 1) * 512],
                        start=(c == 0), stop=(c == DC - 1)),
                        waits=wv, tag=(nt == NT - 1))
                wbuf_use[j] = mmc
                mmprev = mmc
            for nt in range(NT):
                wv = [mmprev]
                if nt == 0 and truk_prev[0] is not None:
                    wv.append(truk_prev[0])
                tg = p.op("scalar", lambda s, nt=nt, dst=dst: s.copy(
                    out=dst[:, nt * 512:(nt + 1) * 512], in_=ps[nt][:, :]),
                    waits=tuple(wv))
                bank_rd[nt] = tg
        t_lu = p.dma("gpsimd", lu_sb[:, :], luT[:, hsl], waits=(p.last("tensor"),))
        for nt in range(NT):
            p.op("tensor", lambda t, nt=nt: t.matmul(
                ps[nt][:, :], lu_sb[:, :],
                t1_sb[:, nt * 512:(nt + 1) * 512], start=True, stop=True),
                waits=((t_lu,) if nt == 0 else ()) + bank_guard(nt))
        t_mm_w = p.last("tensor")
        for nt in range(NT):
            sl = slice(nt * 512, (nt + 1) * 512)
            p.op("scalar", lambda s, nt=nt, h2=h2: s.activation(
                out=za, in_=ps[nt][:, :], func=AF.Abs, bias=wbcol[:, h2:h2 + 1]),
                waits=(t_mm_w, t_wb) +
                ((t_stt_prev[0],) if t_stt_prev[0] else ()))
            p.op("scalar", lambda s: s.activation(
                out=za, in_=za, func=AF.Exp, scale=-1.0))
            p.op("scalar", lambda s: s.activation(
                out=za, in_=za, func=AF.Ln, bias=onec[:, :]), waits=(t_one,))
            tr_ = p.op("scalar", lambda s, nt=nt, h2=h2: s.activation(
                out=rr, in_=ps[nt][:, :], func=AF.Relu, bias=wbcol[:, h2:h2 + 1]))
            wv = [tr_]
            if nt == 0 and afree[0] is not None:
                wv.append(afree[0])
            tg = p.op("vector", lambda v, sl=sl: v.scalar_tensor_tensor(
                A_f[:, sl], rr, -1.0, za, ALU.mult, ALU.subtract),
                waits=tuple(wv))
            t_stt_prev[0] = tg
            bank_rd[nt] = tr_

    def plan_tail(h2):
        hsl = slice(h2 * 128, (h2 + 1) * 128)
        buf_r = stg2[h2 % 2]
        buf_k = stg2[(h2 + 1) % 2]
        for b2 in range(NB):
            for ch in range(NCH):
                sl = slice(b2 * T + ch * CHK, b2 * T + (ch + 1) * CHK)
                p.op("vector", lambda v, sl=sl: v.tensor_tensor_scan(
                    A_f[:, sl], A_f[:, sl], zeros128[:, :], 0.0, ALU.add,
                    ALU.add), waits=(t_z,))
        t_scan = p.last("vector")
        t_eA = p.op("scalar", lambda s: s.activation(out=eA, in_=A_f,
                                                     func=AF.Exp),
                    waits=(t_scan,) +
                    ((eAreg_rd[0],) if eAreg_rd[0] is not None else ()))
        eA3 = eA.rearrange("p (s c) -> p s c", c=CHK)
        t_EC = p.op("vector", lambda v, h2=h2: v.tensor_copy(
            ECall[:, h2 * 32:(h2 + 1) * 32],
            eA3[:, :, CHK - 1:CHK].rearrange("p s c -> p (s c)")), waits=(t_eA,))
        p.op("vector", lambda v, buf_r=buf_r: v.tensor_tensor(
            buf_r[:, 1:NTOK], r_t[:, 1:NTOK], eA[:, 0:NTOK - 1], ALU.mult),
            waits=(t_eA,) +
            ((st_dma[h2 % 2],) if h2 % 2 in st_dma else ()))
        r3 = r_t.rearrange("p (s c) -> p s c", c=CHK)
        s3v = buf_r.rearrange("p (s c) -> p s c", c=CHK)
        tfix = p.op("vector", lambda v, s3v=s3v, r3=r3: v.tensor_copy(
            s3v[:, :, 0:1], r3[:, :, 0:1]))
        d_rt = p.dma("sync", rt_dr[hsl, :], buf_r[:, :], waits=(tfix,))
        t_eN = p.op("scalar", lambda s: s.activation(out=eN, in_=A_f,
                                                     func=AF.Exp, scale=-1.0),
                    waits=(t_EC, tfix))
        afree[0] = t_eN
        tkt = p.op("vector", lambda v, buf_k=buf_k: v.tensor_tensor(
            buf_k[:, :], k_t[:, :], eN, ALU.mult),
            waits=(t_eN,) +
            ((st_dma[(h2 + 1) % 2],) if (h2 + 1) % 2 in st_dma else ()))
        d_kt = p.dma("sync", kt_dr[hsl, :], buf_k[:, :], waits=(tkt,))
        st_dma[(h2 + 1) % 2] = d_kt
        ec3 = ECall[:, h2 * 32:(h2 + 1) * 32].rearrange(
            "p (s o) -> p s o", o=1).broadcast_to((128, 32, CHK))
        eN3 = eN.rearrange("p (s c) -> p s c", c=CHK)
        p.op("vector", lambda v, eN3=eN3, ec3=ec3:
             v.tensor_tensor(eN3, eN3, ec3, ALU.mult))
        k3o = buf_r.rearrange("p (s c) -> p s c", c=CHK)
        tkh = p.op("vector", lambda v, k3o=k3o: v.tensor_tensor(
            k3o, k_t.rearrange("p (s c) -> p s c", c=CHK), eN3, ALU.mult),
            waits=(d_rt,))
        eAreg_rd[0] = tkh
        d_kh = p.dma("sync", kh_dr[hsl, :], buf_r[:, :], waits=(tkh,))
        st_dma[h2 % 2] = d_kh
        truk = p.op("vector", lambda v, h2=h2: v.scalar_tensor_tensor(
            ruk[:, :], r_t[:, :], ucol[:, h2:h2 + 1], k_t[:, :],
            ALU.mult, ALU.mult),
            waits=(t_uc,) + ((dmm_prev[0],) if dmm_prev[0] is not None else ()))
        truk_prev[0] = truk
        dmm = None
        for b2 in range(NB):
            for ch in range(NCH):
                sidx = b2 * NCH + ch
                nsl = slice(b2 * T + ch * CHK, b2 * T + (ch + 1) * CHK)
                for g in range(2):
                    rsl = slice(g * 64, (g + 1) * 64)
                    dmm = p.op("tensor", lambda t, rsl=rsl, nsl=nsl, g=g,
                               sidx=sidx:
                               t.matmul(ps[6][:, g * 32 + sidx: g * 32 + sidx + 1],
                                        ruk[rsl, nsl], onescol_bf[rsl, 0:1],
                                        start=True, stop=True),
                               waits=((truk, t_oc) + bank_guard(6)
                                      if sidx + g == 0 else ()),
                               tag=(g == 1 and sidx == NB * NCH - 1))
        dmm_prev[0] = dmm
        tg = p.op("vector", lambda v, h2=h2: v.tensor_copy(
            Dall[:, h2 * 64:(h2 + 1) * 64], ps[6][:, 0:64]), waits=(dmm,))
        bank_rd[6] = tg

    for h2 in range(DC):
        if h2 >= 1:
            plan_tail(h2 - 1)
        plan_front(h2)
    plan_tail(DC - 1)
    pbC = p.phase_barrier()

    p.arena_reset()
    wv_sb = p.arena(DC * D, BF16)
    wg_sb = p.arena(DC * D, BF16)
    wld = []
    for c in range(DC):
        wld.append(p.dma("gpsimd", wv_sb[:, c * D:(c + 1) * D],
                         wvT[c * 128:(c + 1) * 128, :], waits=pbC if c == 0 else ()))
        wld.append(p.dma("gpsimd", wg_sb[:, c * D:(c + 1) * D],
                         wgT[c * 128:(c + 1) * 128, :], waits=()))
    vg_stage = p.arena(D, BF16)
    vg_stage2 = p.arena(D, BF16)
    d_v = d_g = None
    for tt_ in range(NTOK // 128):
        ntc = tt_ % NT
        for c in range(DC):
            base = c * NTOK + tt_ * 128
            wv = (pbC + tuple(wld) + (t_xb,)) if (tt_ == 0 and c == 0) else \
                 (tuple(wld),) and ()
            p.op("tensor", lambda t, base=base, c=c: t.matmul(
                ps[0][:, :], xb[:, base:base + 128], wv_sb[:, c * D:c * D + 512],
                start=(c == 0), stop=(c == DC - 1)),
                waits=((bank_guard(0) + tuple(wld) + (t_xb,)) if c == 0 else ()),
                tag=False)
            p.op("tensor", lambda t, base=base, c=c: t.matmul(
                ps[1][:, :], xb[:, base:base + 128], wv_sb[:, c * D + 512:c * D + 1024],
                start=(c == 0), stop=(c == DC - 1)),
                waits=(bank_guard(1) if c == 0 else ()), tag=False)
            p.op("tensor", lambda t, base=base, c=c: t.matmul(
                ps[2][:, :], xb[:, base:base + 128], wg_sb[:, c * D:c * D + 512],
                start=(c == 0), stop=(c == DC - 1)),
                waits=(bank_guard(2) if c == 0 else ()), tag=False)
            p.op("tensor", lambda t, base=base, c=c: t.matmul(
                ps[3][:, :], xb[:, base:base + 128], wg_sb[:, c * D + 512:c * D + 1024],
                start=(c == 0), stop=(c == DC - 1)),
                waits=(bank_guard(3) if c == 0 else ()),
                tag=(c == DC - 1))
        t_mmd = p.last("tensor")
        tv1 = p.op("vector", lambda v: v.tensor_copy(vg_stage[:, 0:512], ps[0][:, :]),
                   waits=(t_mmd,) + ((d_v,) if d_v is not None else ()))
        tv2 = p.op("vector", lambda v: v.tensor_copy(vg_stage[:, 512:1024], ps[1][:, :]))
        bank_rd[0] = tv1
        bank_rd[1] = tv2
        d_v = p.dma("sync", v_dr[tt_ * 128:(tt_ + 1) * 128, :], vg_stage[:, :],
                    waits=(tv2,))
        tg1 = p.op("scalar", lambda s: s.activation(
            out=vg_stage2[:, 0:512], in_=ps[2][:, :], func=AF.Sigmoid),
            waits=(t_mmd,) + ((d_g,) if d_g is not None else ()))
        tg2 = p.op("scalar", lambda s: s.activation(
            out=vg_stage2[:, 512:1024], in_=ps[3][:, :], func=AF.Sigmoid))
        bank_rd[2] = tg1
        bank_rd[3] = tg2
        d_g = p.dma("sync", g_dr[tt_ * 128:(tt_ + 1) * 128, :], vg_stage2[:, :],
                    waits=(tg2,))
    pbD = p.phase_barrier()

    p.arena_reset()
    stens = [biga_view(DC * NTOK // 2 + 1024 * j, 1024, F32) for j in range(2)]
    rt_ch = [p.arena(1024, BF16) for _ in range(2)]
    kt_ch = [p.arena(1024, BF16) for _ in range(2)]
    kh_ch = [p.arena(1024, BF16) for _ in range(2)]
    v_ch = [p.arena(1024, BF16) for _ in range(2)]
    g_ch = [p.arena(1024, BF16) for _ in range(2)]
    out_chunk = p.arena(1024, F32)
    stens_bf = p.arena(512, BF16)
    sq_scr = p.arena(1024, F32)
    tscr = p.arena(1024, F32)
    ybf = p.arena(1024, BF16)
    khT_sb = p.sbuf("khT_sb", [128, 128], BF16)
    M_sbA = p.sbuf("M_sbA", [128, 128], BF16)
    M_sbB = p.sbuf("M_sbB", [128, 128], BF16)
    st_stats = p.sbuf("st_stats", [128, 64], F32)
    maskU, t_mu_ = p.const_sb("maskU", np.triu(np.ones((128, 128), np.float32), 1))
    grep_sb = p.arena(1024, F32)
    brep_sb = p.arena(1024, F32)
    t_gr = p.dma("sync", grep_sb[:, :], grep_dr[:, :], waits=pbD)
    t_br = p.dma("sync", brep_sb[:, :], brep_dr[:, :], waits=pbD)
    y_fm = biga_view(0, DC * NTOK, BF16)

    tm0 = p.op("vector", lambda v: v.memset(stens[0][:, :], 0.0), waits=pbD)
    ld_prev = {}
    y_done = []
    for ci in range(NCH):
        pin, pout = ci % 2, (ci + 1) % 2
        for b2 in range(NB):
            jj = (ci * NB + b2) % 2
            sidx = b2 * NCH + ci
            n0 = b2 * T + ci * CHK
            wvl = tuple(ld_prev.get(jj, ())) or (pbD if ci == 0 else ())
            lt = []
            for (dr, dstl, qn) in ((rt_dr, rt_ch, "sync"), (kt_dr, kt_ch, "gpsimd"),
                                   (kh_dr, kh_ch, "sync")):
                lt.append(p.dma(qn, dstl[jj],
                                dr[:, n0:n0 + CHK].rearrange("(c p) t -> p c t",
                                                             p=128),
                                waits=wvl))
            lt.append(p.dma("gpsimd", v_ch[jj], v_dr[n0:n0 + CHK, :], waits=wvl))
            lt.append(p.dma("sync", g_ch[jj], g_dr[n0:n0 + CHK, :], waits=wvl))
            lt = tuple(lt)
            tsb = p.op("vector", lambda v, pin=pin, b2=b2: v.tensor_copy(
                stens_bf[:, :], stens[pin][:, b2 * 512:(b2 + 1) * 512]),
                waits=(tm0,))
            ev_list = []
            for h2 in range(DC):
                hsl = slice(h2 * 128, (h2 + 1) * 128)
                ps4bf = ps[4][:, 0:64].bitcast(BF16)
                ttr_ = p.op("tensor", lambda t, jj=jj, hsl=hsl: t.transpose(
                    ps4bf[:, 0:128], kh_ch[jj][:, hsl], identbf),
                    waits=lt + (tidb,) + bank_guard(4))
                tev = p.op("vector", lambda v: v.tensor_copy(
                    khT_sb[:, :], ps4bf[:, 0:128]), waits=(ttr_,))
                bank_rd[4] = tev
                mmA = p.op("tensor", lambda t, jj=jj, hsl=hsl: t.matmul(
                    ps[0][:, 0:128], kt_ch[jj][0:64, hsl], rt_ch[jj][0:64, hsl],
                    start=True, stop=True), waits=lt + bank_guard(0))
                mmB = p.op("tensor", lambda t, jj=jj, hsl=hsl: t.matmul(
                    ps[3][:, 0:128], kt_ch[jj][64:128, hsl], rt_ch[jj][64:128, hsl],
                    start=True, stop=True), waits=lt + bank_guard(3))
                tMA = p.op("vector", lambda v: v.tensor_tensor(
                    M_sbA[:, :], ps[0][:, 0:128], maskU, ALU.mult),
                    waits=(mmA, t_mu_))
                bank_rd[0] = tMA
                tMB = p.op("vector", lambda v: v.tensor_tensor(
                    M_sbB[:, :], ps[3][:, 0:128], maskU, ALU.mult), waits=(mmB,))
                bank_rd[3] = tMB
                for g in range(2):
                    head = 2 * h2 + g
                    rsl = slice(g * 64, (g + 1) * 64)
                    opb = ps[1] if g == 0 else ps[2]
                    spsum = ps[4][0:64, 128:192] if g == 0 else ps[5][0:64, 0:64]
                    spbank = 4 if g == 0 else 5
                    scol = (b2 * DC + h2) * 64
                    Msb = M_sbA if g == 0 else M_sbB
                    tM = tMA if g == 0 else tMB
                    mm1 = p.op("tensor", lambda t, Msb=Msb, jj=jj, head=head,
                               opb=opb:
                               t.matmul(opb[:, 0:64], Msb[:, :],
                                        v_ch[jj][:, head * 64:(head + 1) * 64],
                                        start=True, stop=False),
                               waits=(tM,) + bank_guard(1 if g == 0 else 2),
                               tag=False)
                    mm2 = p.op("tensor", lambda t, jj=jj, rsl=rsl, hsl=hsl,
                               h2=h2, opb=opb:
                               t.matmul(opb[:, 0:64], rt_ch[jj][rsl, hsl],
                                        stens_bf[rsl, h2 * 64:(h2 + 1) * 64],
                                        start=False, stop=True),
                               waits=(tsb,))
                    mm3 = p.op("tensor", lambda t, jj=jj, rsl=rsl, head=head,
                               spsum=spsum:
                               t.matmul(spsum, khT_sb[:, rsl],
                                        v_ch[jj][:, head * 64:(head + 1) * 64],
                                        start=True, stop=True),
                               waits=(tev,) + bank_guard(spbank))
                    tse = p.op("vector", lambda v, rsl=rsl, scol=scol, pin=pin,
                               pout=pout, h2=h2, sidx=sidx, spsum=spsum:
                               v.scalar_tensor_tensor(
                                   stens[pout][rsl, scol:scol + 64],
                                   stens[pin][rsl, scol:scol + 64],
                                   ECall[rsl, h2 * 32 + sidx:h2 * 32 + sidx + 1],
                                   spsum, ALU.mult, ALU.add), waits=(mm3,))
                    bank_rd[spbank] = tse
                    toe = p.op("vector", lambda v, jj=jj, head=head, sidx=sidx,
                               opb=opb:
                               v.scalar_tensor_tensor(
                                   out_chunk[:, head * 64:(head + 1) * 64],
                                   v_ch[jj][:, head * 64:(head + 1) * 64],
                                   Dall[:, head * 32 + sidx:head * 32 + sidx + 1],
                                   opb[:, 0:64], ALU.mult, ALU.add),
                               waits=(mm2, mm1))
                    bank_rd[1 if g == 0 else 2] = toe
                    ev_list.append(toe)
            t_sq = p.op("scalar", lambda s: s.activation(
                out=sq_scr, in_=out_chunk, func=AF.Square), waits=(ev_list[-1],))
            p.op("vector", lambda v: v.tensor_reduce(
                st_stats[:, 0:16].rearrange("p (g b) -> p g b", b=1),
                out_chunk.rearrange("p (g b) -> p g b", b=64), AX, ALU.add))
            p.op("vector", lambda v: v.tensor_reduce(
                st_stats[:, 16:32].rearrange("p (g b) -> p g b", b=1),
                sq_scr.rearrange("p (g b) -> p g b", b=64), AX, ALU.add),
                waits=(t_sq,))
            p.op("vector", lambda v: v.tensor_scalar(
                st_stats[:, 0:16], st_stats[:, 0:16], 1.0 / 64, None, ALU.mult))
            p.op("vector", lambda v: v.tensor_tensor(
                st_stats[:, 32:48], st_stats[:, 0:16], st_stats[:, 0:16], ALU.mult))
            tv_ = p.op("vector", lambda v: v.scalar_tensor_tensor(
                st_stats[:, 16:32], st_stats[:, 16:32], 1.0 / 64,
                st_stats[:, 32:48], ALU.mult, ALU.subtract))
            p.op("scalar", lambda s: s.activation(
                out=st_stats[:, 16:32], in_=st_stats[:, 16:32], func=AF.Ln,
                bias=epsc[:, :]), waits=(tv_, t_eps))
            t_rs = p.op("scalar", lambda s: s.activation(
                out=st_stats[:, 16:32], in_=st_stats[:, 16:32], func=AF.Exp,
                scale=-0.5))
            mu_b = st_stats[:, 0:16].rearrange("p (g b) -> p g b", b=1) \
                .broadcast_to((128, 16, 64))
            rs_b = st_stats[:, 16:32].rearrange("p (g b) -> p g b", b=1) \
                .broadcast_to((128, 16, 64))
            oc3 = out_chunk.rearrange("p (g b) -> p g b", b=64)
            ts3 = tscr.rearrange("p (g b) -> p g b", b=64)
            p.op("vector", lambda v: v.tensor_tensor(ts3, oc3, mu_b, ALU.subtract),
                 waits=(t_rs,))
            p.op("vector", lambda v: v.tensor_tensor(ts3, ts3, rs_b, ALU.mult))
            p.op("vector", lambda v: v.tensor_tensor(tscr, tscr, grep_sb[:, :],
                                                     ALU.mult), waits=(t_gr,))
            p.op("vector", lambda v: v.tensor_tensor(tscr, tscr, brep_sb[:, :],
                                                     ALU.add), waits=(t_br,))
            t_y = p.op("vector", lambda v, jj=jj: v.tensor_tensor(
                ybf, tscr, g_ch[jj][:, :], ALU.mult))
            for c8 in range(DC):
                ps6bf = ps[6][:, 0:64].bitcast(BF16)
                ttb = p.op("tensor", lambda t, c8=c8: t.transpose(
                    ps6bf[:, 0:128], ybf[:, c8 * 128:(c8 + 1) * 128], identbf),
                    waits=(t_y, tidb) + bank_guard(6))
                tyc = p.op("vector", lambda v, c8=c8, n0=n0: v.tensor_copy(
                    y_fm[:, c8 * NTOK + n0: c8 * NTOK + n0 + CHK],
                    ps6bf[:, 0:128]), waits=(ttb,))
                bank_rd[6] = tyc
                y_done.append(tyc)
            ld_prev[jj] = (t_y, p.last("tensor"))
    pbE = p.phase_barrier()

    p.arena_reset()
    wo_sb = p.sbuf("wo_sb", [128, 128], BF16)
    x1t = [p.arena(512, F32) for j in range(2)]
    x2t = [p.arena(512, F32) for j in range(2)]
    stG = {"st": {}, "ld": {}}
    st_last = None
    mmprev = None
    for ot in range(DC):
        for c in range(DC):
            t_w = p.dma("gpsimd", wo_sb[:, :], woT[c * 128:(c + 1) * 128,
                                                   ot * 128:(ot + 1) * 128],
                        waits=(mmprev,) if (ot + c) else pbE)
            for nt in range(NT):
                mmc = p.op("tensor", lambda t, c=c, nt=nt: t.matmul(
                    ps[nt][:, :], wo_sb[:, :],
                    y_fm[:, c * NTOK + nt * 512: c * NTOK + (nt + 1) * 512],
                    start=(c == 0), stop=(c == DC - 1)),
                    waits=(t_w,) + (bank_guard(nt) if c == 0 else ()),
                    tag=(nt == NT - 1))
            mmprev = mmc
        t_mmo = mmprev
        for nt in range(NT):
            j = nt % 2
            nsl = slice(nt * 512, (nt + 1) * 512)
            t_x1 = p.dma("sync", x1t[j],
                         x1_dr[ot * 128:(ot + 1) * 128, nsl],
                         waits=tuple([stG["ld"][j]] if j in stG["ld"] else list(pbE)))
            tv = p.op("vector", lambda v, j=j, nt=nt: v.tensor_tensor(
                x2t[j], ps[nt][:, :], x1t[j], ALU.add),
                waits=(t_mmo, t_x1) + tuple([stG["st"][j]] if j in stG["st"] else []))
            bank_rd[nt] = tv
            stG["ld"][j] = tv
            st_last = p.dma("scalar", x2T[ot * 128:(ot + 1) * 128, nsl],
                            x2t[j], waits=(tv,))
            stG["st"][j] = st_last
    pbG = p.phase_barrier()
    arow2, brow2 = arow1, brow1
    mk1 = mk_dram_streamer(p, x2T, first_waits=pbG)
    t_rows2 = emit_ln_rows(p, mk1, arow2, brow2, ps[0], ps[1], pfx="lng")
    t_ab1 = p.dma("sync", ab2[0:1, :], arow2[:, :], waits=(t_rows2,))
    t_ab2 = p.dma("sync", ab2[1:2, :], brow2[:, :], waits=(t_rows2,))
    mk2 = mk_dram_streamer(p, x2T, first_waits=pbG)
    t_m1p = emit_lnmax(p, mk2, arow2, brow2, m1p, ps, t_rows2, pfx="lmg")
    return p.finalize([st_last, t_ab1, t_ab2, t_m1p])



def _run(nc, in_maps):
    return run_bass_kernel_spmd(nc, in_maps, core_ids=list(range(NCORES))).results


def _cat_scalars(rs, key):
    return np.concatenate([r[key].reshape(1, 1) for r in rs], axis=1)


_PROGS = {}


def _prog(name, builder):
    if name not in _PROGS:
        _PROGS[name] = builder()
    return _PROGS[name]


def kernel(**inputs):
    x = np.asarray(inputs["L0_attn_x"], np.float32)
    w_r = np.asarray(inputs["L0_attn_w_r"], np.float32)
    w_k = np.asarray(inputs["L0_attn_w_k"], np.float32)
    w_v = np.asarray(inputs["L0_attn_w_v"], np.float32)
    w_o = np.asarray(inputs["L0_attn_w_o"], np.float32)
    w_g = np.asarray(inputs["L0_attn_w_g"], np.float32)
    w_base = np.asarray(inputs["L0_attn_w_base"], np.float32)
    w_ld = np.asarray(inputs["L0_attn_w_lora_down"], np.float32)
    w_lu = np.asarray(inputs["L0_attn_w_lora_up"], np.float32)
    u = np.asarray(inputs["L0_attn_u"], np.float32)
    ln_g = np.asarray(inputs["L0_attn_ln_gamma"], np.float32)
    ln_b = np.asarray(inputs["L0_attn_ln_beta"], np.float32)
    w_gate = np.asarray(inputs["L0_ffn_w_gate"], np.float32)
    w_up = np.asarray(inputs["L0_ffn_w_up"], np.float32)
    w_down = np.asarray(inputs["L0_ffn_w_down"], np.float32)

    C = np.ascontiguousarray
    xTs = [C(x[2 * c:2 * c + 2].transpose(2, 0, 1).reshape(D, NTOK))
           for c in range(NCORES)]
    wupT = C(w_up.T)
    wgateT = C(w_gate.T)
    wdnT = C(w_down.T)
    attn_common = {
        "wdnN": w_down, "wdnT": wdnT,
        "wrT": C(w_r.T), "wkT": C(w_k.T), "wvT": C(w_v.T), "wgT": C(w_g.T),
        "woT": C(w_o.T), "ldT": C(w_ld.T), "luT": C(w_lu.T),
        "wb_t": C(w_base.reshape(DC, 128).T), "u_t": C(u.reshape(DC, 128).T),
        "grep": C(np.tile(ln_g, (128, NHEAD))), "brep": C(np.tile(ln_b, (128, NHEAD))),
    }

    def ffn_half(xT_list, ab_list, m1all):
        nc2 = _prog("L2", build_L2)
        r2 = _run(nc2, [{"xT": xT_list[c], "ab": ab_list[c], "m1all": m1all,
                         "wupN": w_up, "wupT": wupT,
                         "wgateN": w_gate, "wgateT": wgateT}
                        for c in range(NCORES)])
        m2all = C(np.concatenate([r["m23"][:, 0:1] for r in r2], axis=1))
        m3all = C(np.concatenate([r["m23"][:, 1:2] for r in r2], axis=1))
        nc3 = _prog("L3", build_L3)
        r3 = _run(nc3, [{"gate_raw": r2[c]["gate_raw"], "m3all": m3all}
                        for c in range(NCORES)])
        m4all = _cat_scalars(r3, "m4")
        nc4 = _prog("L4", build_L4)
        r4 = _run(nc4, [{"GR_raw": r3[c]["GR_raw"], "up_raw": r2[c]["up_raw"],
                         "m2all": m2all, "m3all": m3all, "m4all": m4all}
                        for c in range(NCORES)])
        m5all = _cat_scalars(r4, "m5")
        op_list = [r4[c]["OP_raw"] for c in range(NCORES)]
        return op_list, m2all, m3all, m4all, m5all

    nc1 = _prog("L1", build_L1)
    r1 = _run(nc1, [{"xT": xTs[c]} for c in range(NCORES)])
    m1all = _cat_scalars(r1, "m1")
    ab1 = [r1[c]["ab"] for c in range(NCORES)]

    op1, m2a, m3a, m4a, m5a = ffn_half(xTs, ab1, m1all)

    nc5 = _prog("L5", build_L5)
    r5 = _run(nc5, [{**attn_common, "OP_raw": op1[c], "xT": xTs[c],
                     "m2all": m2a, "m3all": m3a, "m4all": m4a, "m5all": m5a}
                    for c in range(NCORES)])
    x2s = [r5[c]["x2T"] for c in range(NCORES)]
    m1pall = _cat_scalars(r5, "m1p")
    ab2 = [r5[c]["ab2"] for c in range(NCORES)]

    op2, m2b, m3b, m4b, m5b = ffn_half(x2s, ab2, m1pall)

    nc9 = _prog("L9", build_L9)
    r9 = _run(nc9, [{"OP_raw": op2[c], "xT": x2s[c], "wdnN": w_down,
                     "wdnT": wdnT, "m2all": m2b, "m3all": m3b,
                     "m4all": m4b, "m5all": m5b} for c in range(NCORES)])

    y = np.empty((16, T, D), np.float32)
    for c in range(NCORES):
        y[2 * c:2 * c + 2] = r9[c]["yT"].reshape(D, NB, T).transpose(1, 2, 0)
    return y
